# version 04
# baseline (speedup 1.0000x reference)
import numpy as np
import concourse.bass as bass
import concourse.mybir as mybir
from concourse.bass_utils import run_bass_kernel_spmd
from contextlib import ExitStack

F32 = mybir.dt.float32
BF16 = mybir.dt.bfloat16
I32 = mybir.dt.int32
AF = mybir.ActivationFunctionType
ALU = mybir.AluOpType
AX = mybir.AxisListType

D = 2048
DI = 4096
NH, HD, DS = 64, 64, 128
XBCD = DI + 2 * DS
DIN = 2 * DI + 2 * DS + NH
DINP = 8576
NM_IP = DINP // 128
IFF = 1408
G, GS, E = 8, 8, 64
TOK = 2048
HALO = 3
TOKH = TOK + HALO
CHK = 128
NCK = TOK // CHK
TH = 1024
EPS = 1e-5

DEBUG_DUMP = [False]
RACE_AUDIT = [False]
RINGP = 2
RINGD = 2
RINGW = 4
RINGO = 2

BUILD_PHASES = "all"


class Ctr:
    def __init__(self, nc, ctx, name):
        self.h = ctx.enter_context(nc.semaphore(name))
        self.n = 0

    def inc(self, inst, k=1):
        inst.then_inc(self.h, k)
        self.n += k
        return inst

    def wait(self, eng, val=None):
        eng.wait_ge(self.h, self.n if val is None else val)


class Chain:
    def __init__(self, nc, ctx, name):
        self.h = ctx.enter_context(nc.semaphore(name))
        self.n = 0

    def op(self, inst):
        inst._wait_ge(self.h, self.n)
        inst.then_inc(self.h, 1)
        self.n += 1
        return self.n


class Ctr2:
    def __init__(self, chain):
        self.chain = chain
        self.h = chain.h
        self.n = 0

    def inc(self, inst, k=1):
        self.n = self.chain.op(inst)
        return inst

    def wait(self, eng, val=None):
        eng.wait_ge(self.h, self.n if val is None else val)


class DmaRing:
    def __init__(self, k, name, nslots):
        self.s = [k.ctr(f"{name}{i}") for i in range(nslots)]

    def issue(self, inst, slot):
        return self.s[slot].inc(inst, 16)

    def wait(self, eng, slot, val=None):
        self.s[slot].wait(eng, val)


class K:
    def __init__(self, nc, ctx):
        self.nc, self.ctx = nc, ctx

    def sb(self, name, shape, dt=F32):
        return self.ctx.enter_context(self.nc.sbuf_tensor(name, shape, dt))

    def ps(self, name, shape, dt=F32):
        return self.ctx.enter_context(self.nc.psum_tensor(name, shape, dt))

    def ctr(self, name):
        return Ctr(self.nc, _ROOT_CTX[0], name)

    def vctr(self, name=None):
        return Ctr2(self.vch)

    def sctr(self, name=None):
        return Ctr2(self.sch)

    def gctr(self, name=None):
        return Ctr2(self.gch)


_ROOT_CTX = [None]


def build(phases=BUILD_PHASES, detect_races=True):
    nc = bass.Bass("TRN2", target_bir_lowering=False, num_devices=8,
                   detect_race_conditions=detect_races)
    ctx = ExitStack()
    _ROOT_CTX[0] = ctx

    def din(name, shape, dt=F32):
        return nc.dram_tensor(name, shape, dt, kind="ExternalInput").ap()

    t = {}
    t["u"] = din("u", [D, TOK])
    t["guw"] = din("guw", [G, 11, 2, 128, 2048])
    t["dnw"] = din("dnw", [G, 16, 128, IFF])
    t["winw"] = din("winw", [NM_IP, 128, 2048])
    t["woutw"] = din("woutw", [16, 128, DI])
    t["gatet"] = din("gatet", [128, 16 * E])
    t["mgt"] = din("mgt", [128, G * 16 * GS])
    t["convw"] = din("convw", [128, 34 * 4])
    t["convb"] = din("convb", [128, 34])
    t["dtb"] = din("dtb", [NH, 1])
    t["alog"] = din("alog", [NH, 1])
    t["dexp"] = din("dexp", [128, 32])
    t["nw"] = din("nw", [128, 32])
    t["drow"] = din("drow", [1, NH])
    t["invm"] = din("invm", [1, 64], I32)
    t["identf"] = din("identf", [128, 128])
    t["identb"] = din("identb", [128, 128], BF16)
    t["utm"] = din("utm", [128, 128])
    t["ones2d"] = din("ones2d", [128, 128])
    t["onesr"] = din("onesr", [1, 128])
    t["onecb"] = din("onecb", [128, 1], BF16)
    t["bdiag"] = din("bdiag", [64, 8])
    t["bdiagT"] = din("bdiagT", [8, 64])
    t["sel8"] = din("sel8", [8, G * 128])
    t["role"] = din("role", [1, 1], I32)
    t["rolei"] = din("rolei", [1, 1], I32)
    t["nonce"] = din("nonce", [1, 2], I32)
    t["smask"] = din("smask", [128, 1])

    t["out"] = nc.dram_tensor("out", [D, TOK], F32, kind="ExternalOutput").ap()

    def dsc(name, shape, dt=F32, shared=False):
        return nc.dram_tensor(name, shape, dt, kind="Internal",
                              addr_space="Shared" if shared else "Local").ap()

    shr = not RACE_AUDIT[0]
    t["xp_sh"] = dsc("xp_sh", [D, 2 * TOK], shared=shr)
    t["st_sh"] = dsc("st_sh", [DS, NH * HD], shared=shr)
    t["fx_sh"] = dsc("fx_sh", [1, 1], I32, shared=shr)
    t["fs_sh"] = dsc("fs_sh", [1, 1], I32, shared=shr)
    t["xa_s"] = dsc("xa_s", [D, TOK])
    t["xa2_s"] = dsc("xa2_s", [D, TOK])
    t["xl_s"] = dsc("xl_s", [D, TOKH])
    t["xbc_s"] = dsc("xbc_s", [XBCD, TOKH])
    t["zs_s"] = dsc("zs_s", [DI, TOKH])
    t["xtm_s"] = dsc("xtm_s", [NCK, 128, DI], BF16)
    t["btm_s"] = dsc("btm_s", [NCK, 128, DS], BF16)
    t["x2_s"] = dsc("x2_s", [D, TOK])
    t["pq_s"] = dsc("pq_s", [G, 8, 2816], BF16)
    t["sf_d"] = dsc("sf_d", [64, TH], BF16)
    if DEBUG_DUMP[0]:
        t["ydbg_s"] = dsc("ydbg_s", [NCK, 128, DI])
        t["tdbg_s"] = dsc("tdbg_s", [4, 128, 32 * 512], BF16)
        t["pdbg_s"] = dsc("pdbg_s", [NCK, 128, NH * HD])

    with nc.Block():
        k = K(nc, ctx)
        emit(k, t, phases)
    ctx.close()
    return nc


def barrier(k, name, ctrs=()):
    nc = k.nc
    b = k.ctr("bar_" + name)
    engs = [nc.tensor, nc.vector, nc.scalar, nc.gpsimd, nc.sync]
    for e in engs:
        for c in ctrs:
            c.wait(e)
        e.nop().then_inc(b.h, 1)
    b.n = 5
    for e in engs:
        b.wait(e)


def emit(k, t, phases):
    nc = k.nc
    k.vch = Chain(nc, _ROOT_CTX[0], "chainV")
    k.sch = Chain(nc, _ROOT_CTX[0], "chainS")
    k.gch = Chain(nc, _ROOT_CTX[0], "chainG")

    k.identf = k.sb("c_identf", [128, 128])
    k.identb = k.sb("c_identb", [128, 128], BF16)
    k.utm = k.sb("c_utm", [128, 128])
    k.ones2d = k.sb("c_ones2d", [128, 128])
    k.onesr = k.sb("c_onesr", [1, 128])
    k.onecb = k.sb("c_onecb", [128, 1], BF16)
    k.bdiag = k.sb("c_bdiag", [64, 8])
    k.bdiagT = k.sb("c_bdiagT", [8, 64])
    k.sel8 = k.sb("c_sel8", [8, G * 128])
    k.gatet = k.sb("c_gatet", [128, 16 * E], BF16)
    k.mgt = k.sb("c_mgt", [128, G * 16 * GS], BF16)
    k.convw = k.sb("c_convw", [128, 34 * 4])
    k.convb = k.sb("c_convb", [128, 34])
    k.dtb = k.sb("c_dtb", [NH, 1])
    k.nA = k.sb("c_nA", [NH, 1])
    k.dexp = k.sb("c_dexp", [128, 32])
    k.nw = k.sb("c_nw", [128, 32])
    k.drow = k.sb("c_drow", [1, NH])
    k.invr = k.sb("c_invr", [1, 64], I32)
    k.invf = k.sb("c_invf", [1, 64])
    k.iota = k.sb("c_iota", [64, 1])
    k.pmat = k.sb("c_pmat", [64, 64])
    k.rolev = k.sb("c_role", [1, 1], I32)
    k.roleiv = k.sb("c_rolei", [1, 1], I32)
    k.noncev = k.sb("c_nonce", [1, 2], I32)
    k.smask = k.sb("c_smask", [128, 1])
    k.zero3 = k.sb("c_zero3", [128, 48])
    k.cbias = k.sb("c_cbias", [128, 2])
    k.dbc = k.sb("c_dbc", [128, NH])
    alog_t = k.sb("c_alog", [NH, 1])

    cl = k.ctr("c_ld")
    for dst, src in ((k.identf, t["identf"]), (k.utm, t["utm"]),
                     (k.ones2d, t["ones2d"]), (k.onesr, t["onesr"]),
                     (k.bdiag, t["bdiag"]), (k.bdiagT, t["bdiagT"]),
                     (k.sel8, t["sel8"]), (k.convw, t["convw"]),
                     (k.convb, t["convb"]), (k.dtb, t["dtb"]),
                     (alog_t, t["alog"]), (k.dexp, t["dexp"]),
                     (k.nw, t["nw"]), (k.drow, t["drow"]),
                     (k.invr, t["invm"]), (k.rolev, t["role"]),
                     (k.roleiv, t["rolei"]), (k.noncev, t["nonce"]),
                     (k.smask, t["smask"]), (k.identb, t["identb"]),
                     (k.onecb, t["onecb"])):
        cl.inc(nc.sync.dma_start(out=dst[:], in_=src), 16)
    clg = k.ctr("c_ldg")
    clg.inc(nc.gpsimd.dma_start(out=k.gatet[:], in_=t["gatet"]), 16)
    clg.inc(nc.gpsimd.dma_start(out=k.mgt[:], in_=t["mgt"]), 16)

    cd = k.vctr()
    cs = k.sctr()
    tin = k.ctr("c_tin")
    pictx = ExitStack()
    pinit = K(nc, pictx).ps("p_init", [128, 128])
    cl.wait(nc.vector)
    cd.inc(nc.vector.memset(k.zero3[:], 0.0))
    cd.inc(nc.vector.memset(k.cbias[:, 0:1], EPS))
    cd.inc(nc.vector.memset(k.cbias[:, 1:2], 1.0))
    cd.inc(nc.vector.tensor_copy(out=k.invf[:], in_=k.invr[:]))
    cd.inc(nc.vector.reduce_sum(out=k.iota[0:64, 0:1], in_=k.utm[0:64, :], axis=AX.X))
    cd.inc(nc.vector.tensor_scalar(out=k.iota[0:64, 0:1], in0=k.iota[0:64, 0:1],
                                   scalar1=-1.0, scalar2=128.0, op0=ALU.mult, op1=ALU.add))
    cl.wait(nc.scalar)
    cs.inc(nc.scalar.activation(out=k.nA[:], in_=alog_t[:], func=AF.Exp))
    cs.wait(nc.vector)
    cd.inc(nc.vector.tensor_scalar(out=k.nA[:], in0=k.nA[:], scalar1=-1.0,
                                   scalar2=None, op0=ALU.mult))
    cd.wait(nc.tensor)
    tin.inc(nc.tensor.matmul(pinit[0:64, 0:64], k.onesr[0:1, 0:64], k.invf[0:1, :],
                             start=True, stop=True))
    tin.wait(nc.vector)
    cd.inc(nc.vector.tensor_scalar(out=k.pmat[:], in0=pinit[0:64, 0:64],
                                   scalar1=k.iota[0:64, 0:1], scalar2=None,
                                   op0=ALU.is_equal))
    cd.wait(nc.tensor)
    tin.inc(nc.tensor.matmul(pinit[:, 0:NH], k.onesr[0:1, :], k.drow[0:1, :],
                             start=True, stop=True))
    tin.wait(nc.vector)
    cd.inc(nc.vector.tensor_copy(out=k.dbc[:], in_=pinit[:, 0:NH]))

    pictx.close()
    barrier(k, "init", [cl, clg, cd, cs, tin])

    if phases == "ffn1":
        emit_ffn(k, t, call=0, src=t["u"], acc=t["xa2_s"])
        fin = k.ctr("fincp")
        fin.inc(nc.sync.dma_start(out=t["out"], in_=t["xa2_s"]), 16)
        fin.wait(nc.sync)
        barrier(k, "fin")
        return
    emit_ffn(k, t, call=0, src=t["u"], acc=t["xa_s"])
    emit_handoff_x(k, t)
    barrier(k, "xl")
    with ExitStack() as ms:
        km = K(nc, ms)
        k.dtc_tm = km.sb("m_dtctm", [128, NCK * 64])
        k.cum_tm = km.sb("m_cumtm", [128, NCK * 64])
        k.e_tm = km.sb("m_etm", [128, NCK * 64])
        k.bias_tm = km.sb("m_biastm", [128, NCK * 64])
        k.w_tm = km.sb("m_wtm", [128, NCK * 64])
        k.atot = km.sb("m_atot", [128, NCK * 64])
        with ExitStack() as msf:
            kf = K(nc, msf)
            emit_inproj(k, t, kf)
            barrier(k, "ip")
            if phases == "inproj":
                return
            emit_dt(k, t, km)
            barrier(k, "dt")
        emit_pass12(k, t, km)
        barrier(k, "p2")
    if phases == "mamba":
        return
    emit_ffn(k, t, call=1, src=t["x2_s"], acc=t["xa2_s"])
    fin = k.ctr("fincp")
    fin.inc(nc.sync.dma_start(out=t["out"], in_=t["xa2_s"]), 16)
    fin.wait(nc.sync)
    barrier(k, "end")



FFN_LIMIT = ["full"]


def emit_ffn(k, t, call, src, acc):
    nc = k.nc
    with ExitStack() as fs:
        kk = K(nc, fs)
        pre = f"f{call}_"
        sb = lambda n, s, d=F32: kk.sb(pre + n, s, d)
        xtl = sb("xtl", [128, 2 * TH])
        xln = sb("xln", [128, 16 * TH], BF16)
        wgu = sb("wgu", [128, 2 * RINGP * 2048], BF16)
        wdn = sb("wdn", [128, RINGD * IFF], BF16)
        abuf = sb("abuf", [128, 2 * 11 * TH], BF16)
        pqb = sb("pqb", [8, 2 * 2816], BF16)
        sbg = sb("sbg", [128, 2 * TH], BF16)
        tmp1 = sb("tmp1", [128, 2 * 512])
        tmp2 = sb("tmp2", [128, 2 * 512])
        xbr = sb("xbr", [128, TH], BF16)
        rbm = sb("rbm", [128, 2 * TH])
        srow = sb("srow", [1, 2 * TH])
        lgs = sb("lgs", [64, 512])
        lgt = sb("lgt", [128, 4 * 64])
        m8 = sb("m8", [128, 8])
        rep = sb("rep", [128, 4 * 64])
        wet = sb("wet", [128, 4 * 64])
        wef = sb("wef", [64, TH])
        fl64 = sb("fl64", [64, TH])
        rcp8 = sb("rcp8", [8, TH])
        scl8 = sb("scl8", [8, TH])
        soft8 = sb("soft8", [8, G * TH], BF16)
        sm1 = sb("sm1", [128, 4])
        bnc = rbm

        xl_ld = DmaRing(k, pre + "xld", 2)
        xa_ld = DmaRing(k, pre + "xald", 2)
        xa_st = DmaRing(k, pre + "xast", 2)
        cpy = k.ctr(pre + "cpy")
        gu_ld = DmaRing(k, pre + "guld", RINGP)
        dn_ld = DmaRing(k, pre + "dnld", RINGD)
        pq_ld = DmaRing(k, pre + "pqld", 2)
        pq_st = k.ctr(pre + "pqst")
        mmg = k.ctr(pre + "mmg")
        mmu = k.ctr(pre + "mmu")
        mmd = k.ctr(pre + "mmd")
        ssil = k.sctr()
        vt2 = k.vctr()
        vab = k.vctr()
        vrm = k.vctr()
        tsb = k.ctr(pre + "tsb")
        vsb = k.vctr()
        tpq = k.ctr(pre + "tpq")
        vpq = k.vctr()

        cpy.inc(nc.sync.dma_start(out=acc, in_=src), 16)

        for half in range(2):
            first = (call == 0 and half == 0)
            c0 = half * TH
            mmu0, mmd0, mmg0 = mmu.n, mmd.n, mmg.n
            ab_done = {}

            gu_mark, dn_mark, pq_mark = {}, {}, {}
            if FFN_LIMIT[0] == "A":
                pass
            elif not first:
                pq_st.wait(nc.gpsimd)
            for p in range(4):
                if FFN_LIMIT[0] == "A":
                    break
                for g in (2 * p, 2 * p + 1):
                    if not first:
                        if g >= 2:
                            nc.gpsimd.wait_ge(mmu.h, mmu0 + 22 * (g - 1))
                        pq_ld.issue(nc.gpsimd.dma_start(
                            out=pqb[0:8, (g % 2) * 2816:(g % 2) * 2816 + 2816],
                            in_=t["pq_s"][g]), g % 2)
                        pq_mark[g] = pq_ld.s[g % 2].n
                    for mb in range(11):
                        u = g * 11 + mb
                        if u >= RINGP:
                            nc.gpsimd.wait_ge(mmu.h, mmu0 + 2 * (u - RINGP + 1))
                        sl = (u % RINGP) * 2
                        gu_ld.issue(nc.gpsimd.dma_start(
                            out=wgu[:, sl * 2048:(sl + 1) * 2048], in_=t["guw"][g, mb, 0]),
                            u % RINGP)
                        gu_ld.issue(nc.gpsimd.dma_start(
                            out=wgu[:, (sl + 1) * 2048:(sl + 2) * 2048], in_=t["guw"][g, mb, 1]),
                            u % RINGP)
                        gu_mark[u] = gu_ld.s[u % RINGP].n
                for hb in range(16):
                    if FFN_LIMIT[0] == "B1":
                        break
                    for gi in range(2):
                        u = (p * 16 + hb) * 2 + gi
                        if u >= RINGD:
                            v = u - RINGD
                            nc.gpsimd.wait_ge(mmd.h, mmd0 + 2 * (v // 2 + 1))
                        sl = u % RINGD
                        dn_ld.issue(nc.gpsimd.dma_start(
                            out=wdn[:, sl * IFF:(sl + 1) * IFF], in_=t["dnw"][2 * p + gi, hb]),
                            sl)
                        dn_mark[u] = dn_ld.s[sl].n

            vA = k.vctr()
            tA = k.ctr(f"{pre}h{half}tA")
            sA = k.sctr()

            if half == 1:
                nc.vector.wait_ge(mmg.h, mmg0)
                nc.vector.wait_ge(mmu.h, mmu0)
                nc.sync.wait_ge(mmg.h, mmg0)
                xa_st.wait(nc.vector, 0)
                xa_st.wait(nc.vector, 1)
            if call == 0 and half == 0:
                cpy.wait(nc.sync)

            xtl_free = [0, 0]
            with ExitStack() as ps1:
                kq = K(nc, ps1)
                ps1b = ExitStack()
                kqb = K(nc, ps1b)
                p_sx = kqb.ps(f"{pre}h{half}sx", [1, TH])
                p_sx2 = kqb.ps(f"{pre}h{half}sx2", [1, TH])
                for kt in range(16):
                    par = kt % 2
                    nc.sync.wait_ge(k.vch.h, xtl_free[par])
                    xl_ld.issue(nc.sync.dma_start(
                        out=xtl[:, par * TH:(par + 1) * TH],
                        in_=src[kt * 128:(kt + 1) * 128, c0:c0 + TH]), par)
                    xl_ld.wait(nc.vector, par)
                    tA.wait(nc.vector)
                    vA.inc(nc.vector.tensor_copy(out=xbr[:, :],
                                                 in_=xtl[:, par * TH:(par + 1) * TH]))
                    xtl_free[par] = vA.n
                    vA.wait(nc.tensor)
                    for n in range(2):
                        tA.inc(nc.tensor.matmul(
                            p_sx[0:1, n * 512:(n + 1) * 512], k.onecb[:, 0:1],
                            xbr[:, n * 512:(n + 1) * 512],
                            start=(kt == 0), stop=(kt == 15)))
                    tA.wait(nc.vector)
                    vA.inc(nc.vector.tensor_tensor(out=xbr[:, :], in0=xbr[:, :],
                                                   in1=xbr[:, :], op=ALU.mult))
                    vA.wait(nc.tensor)
                    for n in range(2):
                        tA.inc(nc.tensor.matmul(
                            p_sx2[0:1, n * 512:(n + 1) * 512], k.onecb[:, 0:1],
                            xbr[:, n * 512:(n + 1) * 512],
                            start=(kt == 0), stop=(kt == 15)))
                tA.wait(nc.vector)
                vA.inc(nc.vector.tensor_scalar(out=srow[0:1, 0:TH], in0=p_sx[0:1, :],
                                               scalar1=1.0 / D, scalar2=None, op0=ALU.mult))
                vA.inc(nc.vector.tensor_scalar(out=srow[0:1, TH:2 * TH], in0=p_sx2[0:1, :],
                                               scalar1=1.0 / D, scalar2=None, op0=ALU.mult))
                vA.inc(nc.vector.tensor_tensor(out=scl8[0:1, 0:TH],
                                               in0=srow[0:1, 0:TH], in1=srow[0:1, 0:TH],
                                               op=ALU.mult))
                vA.inc(nc.vector.tensor_tensor(out=srow[0:1, TH:2 * TH],
                                               in0=srow[0:1, TH:2 * TH],
                                               in1=scl8[0:1, 0:TH], op=ALU.subtract))
                vA.wait(nc.scalar)
                sA.inc(nc.scalar.activation(out=scl8[0:1, 0:TH],
                                            in_=srow[0:1, TH:2 * TH], func=AF.Ln, bias=k.cbias[0:1, 0:1]))
                sA.inc(nc.scalar.activation(out=srow[0:1, TH:2 * TH],
                                            in_=scl8[0:1, 0:TH], func=AF.Exp,
                                            scale=-0.5))
                sA.wait(nc.vector)
                vA.inc(nc.vector.tensor_tensor(out=srow[0:1, 0:TH],
                                               in0=srow[0:1, 0:TH],
                                               in1=srow[0:1, TH:2 * TH], op=ALU.mult))
                vA.inc(nc.vector.tensor_scalar(out=srow[0:1, 0:TH],
                                               in0=srow[0:1, 0:TH],
                                               scalar1=-1.0, scalar2=None, op0=ALU.mult))
                ps1b.close()
                p_rb = kq.ps(f"{pre}h{half}rb", [128, TH])
                p_mrb = kq.ps(f"{pre}h{half}mrb", [128, TH])
                vA.wait(nc.tensor)
                for n in range(2):
                    tA.inc(nc.tensor.matmul(p_rb[:, n * 512:(n + 1) * 512], k.onesr[0:1, :],
                                            srow[0:1, TH + n * 512: TH + (n + 1) * 512],
                                            start=True, stop=True))
                    tA.inc(nc.tensor.matmul(p_mrb[:, n * 512:(n + 1) * 512], k.onesr[0:1, :],
                                            srow[0:1, n * 512:(n + 1) * 512],
                                            start=True, stop=True))
                tA.wait(nc.vector)
                vA.inc(nc.vector.tensor_copy(out=rbm[:, 0:TH], in_=p_rb[:]))
                vA.inc(nc.vector.tensor_copy(out=rbm[:, TH:2 * TH], in_=p_mrb[:]))
                for kt in range(16):
                    par = kt % 2
                    nc.sync.wait_ge(k.vch.h, xtl_free[par])
                    xl_ld.issue(nc.sync.dma_start(
                        out=xtl[:, par * TH:(par + 1) * TH],
                        in_=src[kt * 128:(kt + 1) * 128, c0:c0 + TH]), par)
                    xl_ld.wait(nc.vector, par)
                    vA.inc(nc.vector.tensor_tensor(out=xtl[:, par * TH:(par + 1) * TH],
                                                   in0=xtl[:, par * TH:(par + 1) * TH],
                                                   in1=rbm[:, 0:TH], op=ALU.mult))
                    vA.inc(nc.vector.tensor_tensor(out=xln[:, kt * TH:(kt + 1) * TH],
                                                   in0=xtl[:, par * TH:(par + 1) * TH],
                                                   in1=rbm[:, TH:2 * TH], op=ALU.add))
                    xtl_free[par] = vA.n
            vln_done = vA.n

            with ExitStack() as ps3:
                kq = K(nc, ps3)
                p_lg = kq.ps(f"{pre}h{half}lg", [64, 512])
                p_tr = kq.ps(f"{pre}h{half}tr", [128, 128])
                p_g8 = kq.ps(f"{pre}h{half}g8", [8, 512])
                p_x64 = kq.ps(f"{pre}h{half}x64", [64, 512])
                for n in range(2):
                    nc.tensor.wait_ge(vA.h, vln_done)
                    if n == 1:
                        vA.wait(nc.tensor)
                    for kt in range(16):
                        tA.inc(nc.tensor.matmul(
                            p_lg[:, :], k.gatet[:, kt * 64:(kt + 1) * 64],
                            xln[:, kt * TH + n * 512: kt * TH + (n + 1) * 512],
                            start=(kt == 0), stop=(kt == 15)))
                    tA.wait(nc.vector)
                    vA.inc(nc.vector.tensor_copy(out=lgs[:, :], in_=p_lg[:]))
                    for st in range(4):
                        vA.wait(nc.tensor)
                        tA.inc(nc.tensor.transpose(
                            p_tr[0:128, 0:64], lgs[0:64, st * 128:(st + 1) * 128],
                            k.identf[0:64, 0:64]))
                        tA.wait(nc.vector)
                        vA.inc(nc.vector.tensor_copy(out=lgt[:, st * 64:(st + 1) * 64],
                                                     in_=p_tr[0:128, 0:64]))
                    for st in range(4):
                        sl_ap = lgt[:, st * 64:(st + 1) * 64]
                        vA.inc(nc.vector.reduce_max(out=sm1[:, 0:1], in_=sl_ap, axis=AX.X,
                                                    negate=True))
                        vA.wait(nc.scalar)
                        sA.inc(nc.scalar.activation(out=rep[:, st * 64:(st + 1) * 64],
                                                    in_=sl_ap, func=AF.Exp, bias=sm1[:, 0:1]))
                        sA.wait(nc.vector)
                        ex = rep[:, st * 64:(st + 1) * 64]
                        vA.inc(nc.vector.reduce_sum(out=sm1[:, 1:2], in_=ex, axis=AX.X))
                        vA.inc(nc.vector.reciprocal(out=sm1[:, 2:3], in_=sm1[:, 1:2]))
                        vA.inc(nc.vector.tensor_scalar(out=sl_ap, in0=ex, scalar1=sm1[:, 2:3],
                                                       scalar2=None, op0=ALU.mult))
                        vA.inc(nc.vector.max(out=m8[:], in_=sl_ap))
                        vA.inc(nc.vector.match_replace(out=ex, in_to_replace=m8[:],
                                                       in_values=sl_ap, imm_value=-1e30))
                        vA.inc(nc.vector.tensor_single_scalar(
                            out=wet[:, st * 64:(st + 1) * 64], in_=ex,
                            scalar=-1e30, op=ALU.is_equal))
                        vA.inc(nc.vector.tensor_mul(out=wet[:, st * 64:(st + 1) * 64],
                                                    in0=wet[:, st * 64:(st + 1) * 64],
                                                    in1=sl_ap))
                    for st in range(4):
                        vA.wait(nc.tensor)
                        tA.inc(nc.tensor.transpose(
                            p_tr[0:64, 0:128], wet[:, st * 64:(st + 1) * 64], k.identf[:]))
                        tA.wait(nc.vector)
                        vA.inc(nc.vector.tensor_copy(
                            out=wef[0:64, n * 512 + st * 128: n * 512 + (st + 1) * 128],
                            in_=p_tr[0:64, 0:128]))
                for n in range(2):
                    vA.wait(nc.tensor)
                    tA.inc(nc.tensor.matmul(p_x64[:, :], k.pmat[:, :],
                                            wef[0:64, n * 512:(n + 1) * 512],
                                            start=True, stop=True))
                    tA.wait(nc.vector)
                    vA.inc(nc.vector.tensor_copy(out=fl64[0:64, n * 512:(n + 1) * 512],
                                                 in_=p_x64[:]))
                for n in range(2):
                    vA.wait(nc.tensor)
                    tA.inc(nc.tensor.matmul(p_g8[:, :], k.bdiag[:, :],
                                            fl64[0:64, n * 512:(n + 1) * 512],
                                            start=True, stop=True))
                    tA.wait(nc.vector)
                    vA.inc(nc.vector.tensor_copy(out=scl8[0:8, n * 512:(n + 1) * 512],
                                                 in_=p_g8[:]))
                vA.inc(nc.vector.tensor_single_scalar(out=wef[0:64, :], in_=fl64[0:64, :],
                                                      scalar=0.0, op=ALU.is_equal))
                vA.inc(nc.vector.scalar_tensor_tensor(out=fl64[0:64, :], in0=wef[0:64, :],
                                                      scalar=-1e9, in1=fl64[0:64, :],
                                                      op0=ALU.mult, op1=ALU.add))
                vA.wait(nc.scalar)
                sA.inc(nc.scalar.activation(out=fl64[0:64, :], in_=fl64[0:64, :], func=AF.Exp))
                for n in range(2):
                    sA.wait(nc.tensor)
                    vA.wait(nc.tensor)
                    tA.inc(nc.tensor.matmul(p_g8[:, :], k.bdiag[:, :],
                                            fl64[0:64, n * 512:(n + 1) * 512],
                                            start=True, stop=True))
                    tA.wait(nc.vector)
                    vA.inc(nc.vector.tensor_scalar(out=rcp8[0:8, n * 512:(n + 1) * 512],
                                                   in0=p_g8[:], scalar1=1e-30,
                                                   scalar2=None, op0=ALU.add))
                    vA.inc(nc.vector.reciprocal(out=rcp8[0:8, n * 512:(n + 1) * 512],
                                                in_=rcp8[0:8, n * 512:(n + 1) * 512]))
                    vA.wait(nc.tensor)
                    tA.inc(nc.tensor.matmul(p_x64[:, :], k.bdiagT[:, :],
                                            rcp8[0:8, n * 512:(n + 1) * 512],
                                            start=True, stop=True))
                    tA.wait(nc.vector)
                    vA.inc(nc.vector.tensor_tensor(out=xbr[0:64, n * 512:(n + 1) * 512],
                                                   in0=fl64[0:64, n * 512:(n + 1) * 512],
                                                   in1=p_x64[:], op=ALU.mult))
                vA.wait(nc.sync)
                xs_r = k.ctr(f"{pre}h{half}shf")
                xs_r.inc(nc.sync.dma_start(out=t["sf_d"], in_=xbr[0:64, :]), 16)
                xs_r.wait(nc.sync)
                xs_r.inc(nc.sync.dma_start(
                    out=soft8[0:8, :].rearrange("s (g q) -> s g q", q=TH),
                    in_=t["sf_d"].rearrange("(g s) q -> s g q", s=8)), 16)
                xs_r.wait(nc.tensor)
                xs_r.wait(nc.vector)
            stA_v, stA_s = vA.n, sA.n
            nc.sync.wait_ge(vA.h, stA_v)
            if FFN_LIMIT[0] == "A":
                continue

            with ExitStack() as psb:
                kq = K(nc, psb)
                p_g = [kq.ps(f"{pre}h{half}pg{i}", [128, 512]) for i in range(2)]
                p_u = [kq.ps(f"{pre}h{half}pu{i}", [128, 512]) for i in range(2)]
                p_d = [kq.ps(f"{pre}h{half}pd{i}", [128, 512]) for i in range(2)]
                p_s = kq.ps(f"{pre}h{half}psb", [128, 512])

                nc.tensor.wait_ge(vA.h, stA_v)
                nc.tensor.wait_ge(sA.h, stA_s)
                nc.vector.wait_ge(sA.h, stA_s)
                vt2.wait(nc.scalar)

                q = 0
                ps_free = 0
                sil_at, t2_at, rm_at = [], [], []
                for p in range(4):
                    for gi in range(2):
                        g = 2 * p + gi
                        for n in range(2):
                            nc.tensor.wait_ge(k.vch.h, ps_free)
                            tsb.inc(nc.tensor.matmul(
                                p_s[:, :], k.sel8[0:8, g * 128:(g + 1) * 128],
                                scl8[0:8, n * 512:(n + 1) * 512],
                                start=True, stop=True))
                            tsb.wait(nc.vector)
                            vsb.inc(nc.vector.tensor_copy(
                                out=sbg[:, (g % 2) * TH + n * 512:(g % 2) * TH + (n + 1) * 512],
                                in_=p_s[:]))
                            ps_free = vsb.n
                        if g >= 2 and FFN_LIMIT[0] != "B1":
                            pv = (g - 2) // 2
                            nc.vector.wait_ge(mmd.h, mmd0 + 32 * (pv + 1))
                        if not first:
                            pq_ld.wait(nc.tensor, g % 2, pq_mark[g])

                        for mb in range(11):
                            u = g * 11 + mb
                            gu_ld.wait(nc.tensor, u % RINGP, gu_mark[u])
                            sl = (u % RINGP) * 2
                            wg_ap = wgu[:, sl * 2048:(sl + 1) * 2048]
                            wu_ap = wgu[:, (sl + 1) * 2048:(sl + 2) * 2048]
                            pofP = (g % 2) * 2816 + mb * 128
                            pofQ = (g % 2) * 2816 + 1408 + mb * 128

                            if first:
                                nc.tensor.wait_ge(k.vch.h, ps_free)
                                for kt in range(16):
                                    tpq.inc(nc.tensor.matmul(
                                        p_s[0:8, 0:128],
                                        k.mgt[:, (g * 16 + kt) * 8:(g * 16 + kt + 1) * 8],
                                        wg_ap[:, kt * 128:(kt + 1) * 128],
                                        start=(kt == 0), stop=(kt == 15)))
                                tpq.wait(nc.vector)
                                vpq.inc(nc.vector.tensor_copy(
                                    out=pqb[0:8, pofP:pofP + 128], in_=p_s[0:8, 0:128]))
                                ps_free = vpq.n
                                pqP_mark = vpq.n
                                nc.tensor.wait_ge(k.vch.h, ps_free)
                                for kt in range(16):
                                    tpq.inc(nc.tensor.matmul(
                                        p_s[0:8, 0:128],
                                        k.mgt[:, (g * 16 + kt) * 8:(g * 16 + kt + 1) * 8],
                                        wu_ap[:, kt * 128:(kt + 1) * 128],
                                        start=(kt == 0), stop=(kt == 15)))
                                tpq.wait(nc.vector)
                                vpq.inc(nc.vector.tensor_copy(
                                    out=pqb[0:8, pofQ:pofQ + 128], in_=p_s[0:8, 0:128]))
                                ps_free = vpq.n
                                pqQ_mark = vpq.n

                            for n in range(2):
                                par = q % 2
                                if q >= 2:
                                    nc.tensor.wait_ge(k.sch.h, sil_at[q - 2])
                                if first:
                                    nc.tensor.wait_ge(k.vch.h, pqP_mark)
                                for kt in range(16):
                                    nc.tensor.matmul(
                                        p_g[par][:, :], wg_ap[:, kt * 128:(kt + 1) * 128],
                                        xln[:, kt * TH + n * 512: kt * TH + (n + 1) * 512],
                                        start=(kt == 0), stop=False)
                                mmg.inc(nc.tensor.matmul(
                                    p_g[par][:, :], pqb[0:8, pofP:pofP + 128],
                                    soft8[0:8, g * TH + n * 512: g * TH + (n + 1) * 512],
                                    start=False, stop=True))
                                if q >= 2:
                                    nc.tensor.wait_ge(k.vch.h, t2_at[q - 2])
                                if first:
                                    nc.tensor.wait_ge(k.vch.h, pqQ_mark)
                                for kt in range(16):
                                    nc.tensor.matmul(
                                        p_u[par][:, :], wu_ap[:, kt * 128:(kt + 1) * 128],
                                        xln[:, kt * TH + n * 512: kt * TH + (n + 1) * 512],
                                        start=(kt == 0), stop=False)
                                mmu.inc(nc.tensor.matmul(
                                    p_u[par][:, :], pqb[0:8, pofQ:pofQ + 128],
                                    soft8[0:8, g * TH + n * 512: g * TH + (n + 1) * 512],
                                    start=False, stop=True))

                                nc.scalar.wait_ge(mmg.h, mmg0 + q + 1)
                                if q >= 2:
                                    nc.scalar.wait_ge(k.vch.h, t2_at[q - 2])
                                ssil.inc(nc.scalar.activation(
                                    out=tmp1[:, par * 512:(par + 1) * 512],
                                    in_=p_g[par][:, :], func=AF.Silu))
                                sil_at.append(ssil.n)
                                ssil.wait(nc.vector)
                                nc.vector.wait_ge(mmu.h, mmu0 + q + 1)
                                vt2.inc(nc.vector.tensor_tensor(
                                    out=tmp2[:, par * 512:(par + 1) * 512],
                                    in0=tmp1[:, par * 512:(par + 1) * 512],
                                    in1=p_u[par][:, :], op=ALU.mult))
                                t2_at.append(vt2.n)
                                a0 = (g % 2) * 11 * TH + mb * TH + n * 512
                                vab.inc(nc.vector.tensor_tensor(
                                    out=abuf[:, a0:a0 + 512],
                                    in0=tmp2[:, par * 512:(par + 1) * 512],
                                    in1=sbg[:, (g % 2) * TH + n * 512:(g % 2) * TH + (n + 1) * 512],
                                    op=ALU.mult))
                                ab_done[g] = vab.n
                                q += 1
                        if first:
                            vpq.wait(nc.gpsimd)
                            pq_st.inc(nc.gpsimd.dma_start(
                                out=t["pq_s"][g],
                                in_=pqb[0:8, (g % 2) * 2816:(g % 2) * 2816 + 2816]), 16)

                    if FFN_LIMIT[0] == "B1":
                        continue
                    nc.tensor.wait_ge(vab.h, ab_done[2 * p + 1])
                    for hb in range(16):
                        for gi2 in range(2):
                            uu = (p * 16 + hb) * 2 + gi2
                            dn_ld.wait(nc.tensor, uu % RINGD, dn_mark[uu])
                        for n in range(2):
                            dpar = (hb * 2 + n) % 2
                            idx = p * 32 + hb * 2 + n
                            if FFN_LIMIT[0] != "B2":
                                if idx < 2:
                                    cpy.wait(nc.sync)
                                if idx >= 2:
                                    nc.sync.wait_ge(k.vch.h, rm_at[idx - 2])
                                if idx >= 32:
                                    xa_st.wait(nc.sync, idx % 2)
                                xa_ld.issue(nc.sync.dma_start(
                                    out=xtl[:, (idx % 2) * TH + 512:(idx % 2) * TH + 1024],
                                    in_=acc[hb * 128:(hb + 1) * 128,
                                            c0 + n * 512:c0 + (n + 1) * 512]), idx % 2)
                            if idx >= 2:
                                nc.tensor.wait_ge(k.vch.h, rm_at[idx - 2])
                            for gi2 in range(2):
                                g2 = 2 * p + gi2
                                sld = ((p * 16 + hb) * 2 + gi2) % RINGD
                                for kb in range(11):
                                    lst = (gi2 == 1 and kb == 10)
                                    a0 = (g2 % 2) * 11 * TH + kb * TH + n * 512
                                    mm = nc.tensor.matmul(
                                        p_d[dpar][:, :],
                                        wdn[:, sld * IFF + kb * 128: sld * IFF + (kb + 1) * 128],
                                        abuf[:, a0:a0 + 512],
                                        start=(gi2 == 0 and kb == 0), stop=lst)
                                    if lst:
                                        mmd.inc(mm)
                            nc.vector.wait_ge(mmd.h, mmd0 + idx + 1)
                            if FFN_LIMIT[0] == "B2":
                                vrm.inc(nc.vector.tensor_copy(
                                    out=bnc[:, (idx % 2) * 512:(idx % 2 + 1) * 512],
                                    in_=p_d[dpar][:, :]))
                                rm_at.append(vrm.n)
                                continue
                            xa_ld.wait(nc.vector, idx % 2)
                            if idx >= 2:
                                xa_st.wait(nc.vector, idx % 2)
                            vrm.inc(nc.vector.tensor_tensor(
                                out=bnc[:, (idx % 2) * 512:(idx % 2 + 1) * 512],
                                in0=xtl[:, (idx % 2) * TH + 512:(idx % 2) * TH + 1024],
                                in1=p_d[dpar][:, :], op=ALU.add))
                            rm_at.append(vrm.n)
                            vrm.wait(nc.sync)
                            xa_st.issue(nc.sync.dma_start(
                                out=acc[hb * 128:(hb + 1) * 128,
                                        c0 + n * 512:c0 + (n + 1) * 512],
                                in_=bnc[:, (idx % 2) * 512:(idx % 2 + 1) * 512]), idx % 2)

        xa_st.wait(nc.sync, 0)
        xa_st.wait(nc.sync, 1)
        barrier(k, f"ffn{call}", [xa_st.s[0], xa_st.s[1], vrm, vab, mmd, ssil, vt2, vsb, pq_st])



def poll_flag(k, flag_dram, flag_sb, nonce_col, tag):
    nc = k.nc
    g = nc.gpsimd
    r_nonce = g.value_load(k.noncev[0:1, nonce_col:nonce_col + 1])
    cnt = g.alloc_register(tag + "_cnt")
    g.reg_mov(cnt, 0)
    fr = g.alloc_register(tag + "_fr")
    pl = k.ctr(tag + "_pl")
    loop, done = tag + "_loop", tag + "_done"
    g.br(loop)
    with nc.body(loop):
        g.reg_add(cnt, cnt, 16)
        g.dma_start(out=flag_sb[:], in_=flag_dram).then_inc(pl.h, 16)
        g.wait_ge(pl.h, cnt)
        g.reg_load(fr, flag_sb[0:1, 0:1])
        g.br_cmp(fr, r_nonce, on_true=done, on_false=loop, comp_op="IS_EQ")
    nc.switch_bb(done)


def emit_handoff_x(k, t):
    nc = k.nc
    g = nc.gpsimd
    flag_sb = k.sb("hx_flag", [1, 1], I32)
    hx = k.ctr("hx")
    xc = k.ctr("hx_copy")
    if RACE_AUDIT[0]:
        hx.inc(g.dma_start(out=t["xp_sh"][:, 0:TOK], in_=t["xa_s"]), 16)
        hx.wait(g)
        xc.inc(g.dma_start(out=t["xl_s"][:, HALO:TOKH], in_=t["xp_sh"][:, 0:TOK]), 16)
        xc.inc(g.dma_start(
            out=t["xl_s"].rearrange("(kt p) n -> p kt n", p=128)[:, :, 0:HALO],
            in_=k.zero3[:].rearrange("p (kt n) -> p kt n", n=3)), 16)
        xc.wait(g)
        xr = k.gctr()
        xr.inc(g.memset(flag_sb[:], 0))
        return
    r = g.value_load(k.rolev[0:1, 0:1])
    with g.If(r):
        g.dma_start(out=t["xp_sh"][:, 0:TOK], in_=t["xa_s"]).then_inc(hx.h, 16)
        g.wait_ge(hx.h, 16)
        g.dma_start(out=t["fx_sh"], in_=k.noncev[0:1, 0:1]).then_inc(hx.h, 16)
        g.wait_ge(hx.h, 32)
    g.end_ifs()
    r2 = g.value_load(k.roleiv[0:1, 0:1])
    with g.If(r2):
        g.dma_start(out=t["xp_sh"][:, TOK:2 * TOK], in_=t["xa_s"]).then_inc(hx.h, 16)
        g.wait_ge(hx.h, 16)
    g.end_ifs()
    poll_flag(k, t["fx_sh"], flag_sb, 0, "pollx")
    r = g.value_load(k.rolev[0:1, 0:1])
    with g.If(r):
        g.dma_start(out=t["xl_s"][:, HALO:TOKH], in_=t["xp_sh"][:, 0:TOK]).then_inc(xc.h, 16)
        g.dma_start(out=t["xl_s"].rearrange("(kt p) n -> p kt n", p=128)[:, :, 0:HALO],
                    in_=k.zero3[:].rearrange("p (kt n) -> p kt n", n=3)).then_inc(xc.h, 16)
        g.wait_ge(xc.h, 32)
    g.end_ifs()
    r2 = g.value_load(k.roleiv[0:1, 0:1])
    with g.If(r2):
        g.dma_start(out=t["xl_s"][:, 0:TOKH],
                    in_=t["xp_sh"][:, TOK - HALO:2 * TOK]).then_inc(xc.h, 16)
        g.wait_ge(xc.h, 16)
    g.end_ifs()
    xr = k.gctr()
    xr.inc(g.memset(flag_sb[:], 0))



def emit_inproj(k, t, km):
    nc = k.nc
    k.dtc_fm = km.sb("m_dtcfm", [64, TOKH])
    k.lndt_fm = km.sb("m_lndtfm", [64, TOKH])
    k.dtcA_fm = km.sb("m_dtcAfm", [64, TOKH])
    with ExitStack() as fs:
        kk = K(nc, fs)
        sb = lambda n, s, d=F32: kk.sb("ip_" + n, s, d)
        xln = sb("xln", [128, 16 * TOKH], BF16)
        xtl = sb("xtl", [128, 2 * 512])
        wmr = sb("wmr", [128, RINGW * 2048], BF16)
        st_in = [sb(f"sin{i}", [128, TOKH]) for i in range(2)]
        st_out = [sb(f"sout{i}", [128, TOKH]) for i in range(2)]
        srow = sb("srow", [1, 3 * 512])
        rbm = sb("rbm", [128, 2 * 512])
        xbr = sb("xbr", [128, 512], BF16)

        xld = DmaRing(k, "ip_xld", 2)
        w_ld = DmaRing(k, "ip_wld", RINGW)
        vA = k.vctr()
        tA = k.ctr("ip_tA")
        sA = k.sctr()
        mmi = k.ctr("ip_mmi")
        evi = k.vctr()
        cvd = k.vctr()
        svd = k.sctr()
        stw = DmaRing(k, "ip_stw", 2)

        NT = [(0, 512), (512, 512), (1024, 512), (1536, 512), (2048, 3)]

        with ExitStack() as ps1:
            kq = K(nc, ps1)
            p_sx = kq.ps("ip_sx", [1, 512])
            p_sx2 = kq.ps("ip_sx2", [1, 512])
            p_rb = kq.ps("ip_rb", [128, 512])
            p_mrb = kq.ps("ip_mrb", [128, 512])
            xtl_free = [0, 0]
            for ni, (n0, w) in enumerate(NT):
                for kt in range(16):
                    par = kt % 2
                    nc.sync.wait_ge(k.vch.h, xtl_free[par])
                    xld.issue(nc.sync.dma_start(
                        out=xtl[:, par * 512:par * 512 + w],
                        in_=t["xl_s"][kt * 128:(kt + 1) * 128, n0:n0 + w]), par)
                    xld.wait(nc.vector, par)
                    tA.wait(nc.vector)
                    vA.inc(nc.vector.tensor_copy(out=xbr[:, 0:w],
                                                 in_=xtl[:, par * 512:par * 512 + w]))
                    xtl_free[par] = vA.n
                    vA.wait(nc.tensor)
                    tA.inc(nc.tensor.matmul(p_sx[0:1, 0:w], k.onecb[:, 0:1], xbr[:, 0:w],
                                            start=(kt == 0), stop=(kt == 15)))
                    tA.wait(nc.vector)
                    vA.inc(nc.vector.tensor_tensor(out=xbr[:, 0:w], in0=xbr[:, 0:w],
                                                   in1=xbr[:, 0:w], op=ALU.mult))
                    vA.wait(nc.tensor)
                    tA.inc(nc.tensor.matmul(p_sx2[0:1, 0:w], k.onecb[:, 0:1], xbr[:, 0:w],
                                            start=(kt == 0), stop=(kt == 15)))
                tA.wait(nc.vector)
                vA.inc(nc.vector.tensor_scalar(out=srow[0:1, 0:w], in0=p_sx[0:1, 0:w],
                                               scalar1=1.0 / D, scalar2=None, op0=ALU.mult))
                vA.inc(nc.vector.tensor_scalar(out=srow[0:1, 512:512 + w],
                                               in0=p_sx2[0:1, 0:w],
                                               scalar1=1.0 / D, scalar2=None, op0=ALU.mult))
                vA.inc(nc.vector.tensor_tensor(out=srow[0:1, 1024:1024 + w],
                                               in0=srow[0:1, 0:w], in1=srow[0:1, 0:w],
                                               op=ALU.mult))
                vA.inc(nc.vector.tensor_tensor(out=srow[0:1, 512:512 + w],
                                               in0=srow[0:1, 512:512 + w],
                                               in1=srow[0:1, 1024:1024 + w], op=ALU.subtract))
                vA.wait(nc.scalar)
                sA.inc(nc.scalar.activation(out=srow[0:1, 1024:1024 + w],
                                            in_=srow[0:1, 512:512 + w], func=AF.Ln, bias=k.cbias[0:1, 0:1]))
                sA.inc(nc.scalar.activation(out=srow[0:1, 512:512 + w],
                                            in_=srow[0:1, 1024:1024 + w], func=AF.Exp,
                                            scale=-0.5))
                sA.wait(nc.vector)
                vA.inc(nc.vector.tensor_tensor(out=srow[0:1, 1024:1024 + w],
                                               in0=srow[0:1, 0:w],
                                               in1=srow[0:1, 512:512 + w], op=ALU.mult))
                vA.inc(nc.vector.tensor_scalar(out=srow[0:1, 1024:1024 + w],
                                               in0=srow[0:1, 1024:1024 + w],
                                               scalar1=-1.0, scalar2=None, op0=ALU.mult))
                vA.wait(nc.tensor)
                tA.inc(nc.tensor.matmul(p_rb[:, 0:w], k.onesr[0:1, :],
                                        srow[0:1, 512:512 + w], start=True, stop=True))
                tA.inc(nc.tensor.matmul(p_mrb[:, 0:w], k.onesr[0:1, :],
                                        srow[0:1, 1024:1024 + w], start=True, stop=True))
                tA.wait(nc.vector)
                vA.inc(nc.vector.tensor_copy(out=rbm[:, 0:w], in_=p_rb[:, 0:w]))
                vA.inc(nc.vector.tensor_copy(out=rbm[:, 512:512 + w], in_=p_mrb[:, 0:w]))
                for kt in range(16):
                    par = kt % 2
                    nc.sync.wait_ge(k.vch.h, xtl_free[par])
                    xld.issue(nc.sync.dma_start(
                        out=xtl[:, par * 512:par * 512 + w],
                        in_=t["xl_s"][kt * 128:(kt + 1) * 128, n0:n0 + w]), par)
                    xld.wait(nc.vector, par)
                    vA.inc(nc.vector.tensor_tensor(out=xtl[:, par * 512:par * 512 + w],
                                                   in0=xtl[:, par * 512:par * 512 + w],
                                                   in1=rbm[:, 0:w], op=ALU.mult))
                    vA.inc(nc.vector.tensor_tensor(
                        out=xln[:, kt * TOKH + n0:kt * TOKH + n0 + w],
                        in0=xtl[:, par * 512:par * 512 + w],
                        in1=rbm[:, 512:512 + w], op=ALU.add))
                    xtl_free[par] = vA.n
        lnv = vA.n

        wi_mark = {}
        for m in range(NM_IP):
            if m >= RINGW:
                nc.gpsimd.wait_ge(mmi.h, 5 * (m - RINGW + 1))
            w_ld.issue(nc.gpsimd.dma_start(
                out=wmr[:, (m % RINGW) * 2048:(m % RINGW + 1) * 2048],
                in_=t["winw"][m]), m % RINGW)
            wi_mark[m] = w_ld.s[m % RINGW].n

        with ExitStack() as ps2:
            kq = K(nc, ps2)
            p_ip = [kq.ps(f"ip_p{i}", [128, 512]) for i in range(2)]
            evi_at = []
            last_g = [None, None]
            for m in range(NM_IP):
                w_ld.wait(nc.tensor, m % RINGW, wi_mark[m])
                if m == 0:
                    nc.tensor.wait_ge(vA.h, lnv)
                sbin = st_in[m % 2]
                if m >= 2:
                    stw.wait(nc.vector, m % 2)
                for ni, (n0, w) in enumerate(NT):
                    par = ni % 2
                    gidx = m * 5 + ni
                    if last_g[par] is not None:
                        nc.tensor.wait_ge(k.vch.h, evi_at[last_g[par]])
                    for kt in range(16):
                        mm = nc.tensor.matmul(
                            p_ip[par][:, 0:w],
                            wmr[:, (m % RINGW) * 2048 + kt * 128:(m % RINGW) * 2048 + (kt + 1) * 128],
                            xln[:, kt * TOKH + n0: kt * TOKH + n0 + w],
                            start=(kt == 0), stop=(kt == 15))
                    mmi.inc(mm)
                    nc.vector.wait_ge(mmi.h, gidx + 1)
                    evi.inc(nc.vector.tensor_copy(out=sbin[:, n0:n0 + w],
                                                  in_=p_ip[par][:, 0:w]))
                    evi_at.append(evi.n)
                    last_g[par] = gidx

                so = st_out[m % 2]
                if m < 32:
                    nc.scalar.wait_ge(evi.h, evi.n)
                    if m >= 2:
                        stw.wait(nc.scalar, m % 2)
                    svd.inc(nc.scalar.activation(out=so[:, :], in_=sbin[:, :], func=AF.Silu))
                    svd.wait(nc.sync)
                    stw.issue(nc.sync.dma_start(out=t["zs_s"][m * 128:(m + 1) * 128, :],
                                                in_=so[:, :]), m % 2)
                elif m < 66:
                    mc = m - 32
                    nc.vector.wait_ge(evi.h, evi.n)
                    cvd.inc(nc.vector.tensor_scalar(
                        out=so[:, HALO:TOKH], in0=sbin[:, 0:TOK],
                        scalar1=k.convw[:, mc * 4:mc * 4 + 1],
                        scalar2=k.convb[:, mc:mc + 1], op0=ALU.mult, op1=ALU.add))
                    for cc in range(1, 4):
                        cvd.inc(nc.vector.scalar_tensor_tensor(
                            out=so[:, HALO:TOKH], in0=sbin[:, cc:TOK + cc],
                            scalar=k.convw[:, mc * 4 + cc:mc * 4 + cc + 1],
                            in1=so[:, HALO:TOKH], op0=ALU.mult, op1=ALU.add))
                    cvd.inc(nc.vector.tensor_copy(out=so[:, 0:HALO], in_=k.zero3[:, 0:HALO]))
                    cvd.wait(nc.scalar)
                    svd.inc(nc.scalar.activation(out=so[:, HALO:TOKH], in_=so[:, HALO:TOKH],
                                                 func=AF.Silu))
                    svd.wait(nc.sync)
                    stw.issue(nc.sync.dma_start(out=t["xbc_s"][mc * 128:(mc + 1) * 128, :],
                                                in_=so[:, :]), m % 2)
                else:
                    nc.vector.wait_ge(evi.h, evi.n)
                    dtb2 = so[0:64, 0:TOKH]
                    cvd.inc(nc.vector.tensor_scalar(out=dtb2, in0=sbin[0:64, :],
                                                    scalar1=k.dtb[0:64, 0:1], scalar2=None,
                                                    op0=ALU.add))
                    cvd.wait(nc.scalar)
                    ab = st_in[(m + 1) % 2][0:64, 0:TOKH]
                    svd.inc(nc.scalar.activation(out=ab, in_=dtb2, func=AF.Abs))
                    svd.inc(nc.scalar.activation(out=ab, in_=ab, func=AF.Exp, scale=-1.0))
                    svd.inc(nc.scalar.activation(out=ab, in_=ab, func=AF.Ln, bias=k.cbias[0:64, 1:2]))
                    svd.wait(nc.vector)
                    cvd.inc(nc.vector.tensor_scalar_max(k.dtc_fm[:, :], dtb2, 0.0))
                    cvd.inc(nc.vector.tensor_add(out=k.dtc_fm[:, :], in0=k.dtc_fm[:, :],
                                                 in1=ab))
                    cvd.wait(nc.scalar)
                    svd.inc(nc.scalar.activation(out=k.lndt_fm[:, :], in_=k.dtc_fm[:, :],
                                                 func=AF.Ln))
                    cvd.inc(nc.vector.tensor_scalar(out=k.dtcA_fm[:, :], in0=k.dtc_fm[:, :],
                                                    scalar1=k.nA[0:64, 0:1], scalar2=None,
                                                    op0=ALU.mult))
                    cvd.wait(nc.sync)
                    svd.wait(nc.sync)
                    stw.issue(nc.sync.dma_start(out=t["xa2_s"][0:1, 0:8],
                                                in_=k.zero3[0:1, 0:8]), m % 2)
        barrier(k, "ipend", [stw.s[0], stw.s[1], cvd, svd, mmi, evi])



def emit_dt(k, t, km):
    nc = k.nc
    with ExitStack() as fs:
        kk = K(nc, fs)
        dtA_tm = kk.sb("dt_dtA", [128, 2 * 64])
        cfm = kk.sb("dt_cfm", [64, 2 * 128])
        dl = kk.sb("dt_dl", [64, 128])
        lnd_tm = kk.sb("dt_lnd", [128, 64])
        arow = kk.sb("dt_arow", [1, 64])
        p_t = kk.ps("dt_pt", [128, 128])
        p_c = kk.ps("dt_pc", [128, 64])
        tD = k.ctr("dt_t")
        vD = k.vctr()
        sD = k.sctr()
        dRow = k.ctr("dt_row")
        for c in range(NCK):
            b0 = HALO + c * CHK
            vD.wait(nc.tensor)
            tD.inc(nc.tensor.transpose(p_t[0:128, 0:64], k.dtc_fm[0:64, b0:b0 + 128],
                                       k.identf[0:64, 0:64]))
            tD.wait(nc.vector)
            vD.inc(nc.vector.tensor_copy(out=k.dtc_tm[:, c * 64:(c + 1) * 64],
                                         in_=p_t[0:128, 0:64]))
            vD.wait(nc.tensor)
            tD.inc(nc.tensor.transpose(p_t[0:128, 0:64], k.dtcA_fm[0:64, b0:b0 + 128],
                                       k.identf[0:64, 0:64]))
            tD.wait(nc.vector)
            vD.inc(nc.vector.tensor_copy(out=dtA_tm[:, (c % 2) * 64:(c % 2) * 64 + 64],
                                         in_=p_t[0:128, 0:64]))
            vD.wait(nc.tensor)
            tD.inc(nc.tensor.transpose(p_t[0:128, 0:64], k.lndt_fm[0:64, b0:b0 + 128],
                                       k.identf[0:64, 0:64]))
            tD.wait(nc.vector)
            vD.inc(nc.vector.tensor_copy(out=lnd_tm[:, :], in_=p_t[0:128, 0:64]))
            vD.wait(nc.tensor)
            tD.inc(nc.tensor.matmul(p_c[:, 0:64], k.utm[:, :],
                                    dtA_tm[:, (c % 2) * 64:(c % 2) * 64 + 64],
                                    start=True, stop=True))
            tD.wait(nc.vector)
            vD.inc(nc.vector.tensor_copy(out=k.cum_tm[:, c * 64:(c + 1) * 64],
                                         in_=p_c[:, 0:64]))
            vD.wait(nc.scalar)
            sD.inc(nc.scalar.activation(out=k.e_tm[:, c * 64:(c + 1) * 64],
                                        in_=k.cum_tm[:, c * 64:(c + 1) * 64], func=AF.Exp))
            vD.inc(nc.vector.tensor_copy(out=k.bias_tm[:, c * 64:(c + 1) * 64],
                                         in_=lnd_tm[:, :]))
            vD.wait(nc.tensor)
            tD.inc(nc.tensor.transpose(p_t[0:64, 0:128], k.cum_tm[:, c * 64:(c + 1) * 64],
                                       k.identf[:, :]))
            tD.wait(nc.vector)
            vD.inc(nc.vector.tensor_copy(out=cfm[0:64, (c % 2) * 128:(c % 2) * 128 + 128],
                                         in_=p_t[0:64, 0:128]))
            cslice = cfm[0:64, (c % 2) * 128:(c % 2) * 128 + 128]
            vD.wait(nc.scalar)
            sD.inc(nc.scalar.activation(out=dl[0:64, :], in_=cslice, func=AF.Exp,
                                        scale=-1.0,
                                        bias=cfm[0:64, (c % 2) * 128 + 127:(c % 2) * 128 + 128]))
            sD.wait(nc.tensor)
            tD.inc(nc.tensor.transpose(p_t[0:128, 0:64], dl[0:64, :], k.identf[0:64, 0:64]))
            tD.wait(nc.vector)
            vD.inc(nc.vector.tensor_copy(out=k.w_tm[:, c * 64:(c + 1) * 64],
                                         in_=p_t[0:128, 0:64]))
            vD.inc(nc.vector.tensor_mul(out=k.w_tm[:, c * 64:(c + 1) * 64],
                                        in0=k.w_tm[:, c * 64:(c + 1) * 64],
                                        in1=k.dtc_tm[:, c * 64:(c + 1) * 64]))
            vD.wait(nc.sync)
            sD.wait(nc.sync)
            dRow.inc(nc.sync.dma_start(
                out=arow[0:1, 0:64],
                in_=cfm[0:64, (c % 2) * 128 + 127:(c % 2) * 128 + 128]), 16)
            dRow.wait(nc.scalar)
            sD.inc(nc.scalar.activation(out=arow[0:1, :], in_=arow[0:1, :], func=AF.Exp))
            sD.wait(nc.tensor)
            vD.wait(nc.tensor)
            tD.inc(nc.tensor.matmul(p_c[:, 0:64], k.onesr[0:1, :], arow[0:1, :],
                                    start=True, stop=True))
            tD.wait(nc.vector)
            vD.inc(nc.vector.tensor_copy(out=k.atot[:, c * 64:(c + 1) * 64],
                                         in_=p_c[:, 0:64]))
        barrier(k, "dtend", [tD, vD, sD, dRow])



def emit_pass12(k, t, km):
    nc = k.nc
    with ExitStack() as fs:
        kk = K(nc, fs)
        sb = lambda n, s, d=F32: kk.sb("ss_" + n, s, d)
        prev = sb("prev", [128, NH * HD])
        xfb = sb("xfb", [128, DI])
        xfb_b = sb("xfbb", [128, DI], BF16)
        bfb = sb("bfb", [128, DS])
        bfb_b = sb("bfbb", [128, DS], BF16)
        xtm = sb("xtm", [128, DI], BF16)
        btm = sb("btm", [128, DS], BF16)
        xw = sb("xw", [128, 2 * HD], BF16)
        sflag = sb("sflag", [1, 1], I32)

        ld1 = k.ctr("p1_ld")
        t1 = k.ctr("p1_t")
        v1 = k.vctr()
        w1 = k.ctr("p1_w")

        with ExitStack() as ps1:
            kq = K(nc, ps1)
            p_tb = [kq.ps(f"p1_tb{i}", [128, 128], BF16) for i in range(2)]
            p_st = [kq.ps(f"p1_st{i}", [128, 64]) for i in range(2)]
            v1.inc(nc.vector.memset(prev[:], 0.0))
            for c in range(NCK):
                b0 = HALO + c * CHK
                if c > 0:
                    nc.sync.wait_ge(v1.h, v1.n)
                    nc.sync.wait_ge(w1.h, w1.n)
                ld1.inc(nc.sync.dma_start(
                    out=xfb[:].rearrange("p (kt n) -> p kt n", n=128),
                    in_=t["xbc_s"][0:DI, b0:b0 + 128].rearrange("(kt p) n -> p kt n", p=128)),
                    16)
                ld1.inc(nc.sync.dma_start(out=bfb[:], in_=t["xbc_s"][DI:DI + DS, b0:b0 + 128]),
                        16)
                ld1.wait(nc.vector)
                if c > 0:
                    nc.vector.wait_ge(t1.h, t1.n)
                v1.inc(nc.vector.tensor_copy(out=xfb_b[:], in_=xfb[:]))
                v1.inc(nc.vector.tensor_copy(out=bfb_b[:], in_=bfb[:]))
                for kt in range(33):
                    par = kt % 2
                    src_ap = xfb_b[:, kt * 128:(kt + 1) * 128] if kt < 32 else bfb_b[:, :]
                    nc.tensor.wait_ge(v1.h, v1.n - (1 if kt >= 2 else 0))
                    t1.inc(nc.tensor.transpose(p_tb[par][:, :], src_ap, k.identb[:, :]))
                    nc.vector.wait_ge(t1.h, t1.n)
                    dst_ap = xtm[:, kt * 128:(kt + 1) * 128] if kt < 32 else btm[:, :]
                    v1.inc(nc.vector.tensor_copy(out=dst_ap, in_=p_tb[par][:, :]))
                v1.wait(nc.sync)
                w1.inc(nc.sync.dma_start(out=t["xtm_s"][c], in_=xtm[:]), 16)
                w1.inc(nc.sync.dma_start(out=t["btm_s"][c], in_=btm[:]), 16)
                for h in range(NH):
                    par = h % 2
                    if h >= 2:
                        nc.vector.wait_ge(t1.h, t1.n)
                    v1.inc(nc.vector.tensor_scalar(
                        out=xw[:, par * HD:(par + 1) * HD], in0=xtm[:, h * HD:(h + 1) * HD],
                        scalar1=k.w_tm[:, c * 64 + h:c * 64 + h + 1], scalar2=None,
                        op0=ALU.mult))
                    nc.tensor.wait_ge(v1.h, v1.n)
                    t1.inc(nc.tensor.matmul(p_st[par][:, :], btm[:, :],
                                            xw[:, par * HD:(par + 1) * HD],
                                            start=True, stop=True))
                    nc.vector.wait_ge(t1.h, t1.n)
                    v1.inc(nc.vector.scalar_tensor_tensor(
                        out=prev[:, h * HD:(h + 1) * HD], in0=prev[:, h * HD:(h + 1) * HD],
                        scalar=k.atot[:, c * 64 + h:c * 64 + h + 1],
                        in1=p_st[par][:, :], op0=ALU.mult, op1=ALU.add))
                if DEBUG_DUMP[0]:
                    pd = k.ctr(f"p1_pd{c}")
                    v1.wait(nc.sync)
                    pd.inc(nc.sync.dma_start(out=t["pdbg_s"][c], in_=prev[:]), 16)
                    pd.wait(nc.vector)

        g = nc.gpsimd
        hs = k.ctr("hs")
        v1.wait(g)
        if RACE_AUDIT[0]:
            hs.inc(g.dma_start(out=t["st_sh"], in_=prev[:]), 16)
            hs.wait(g)
        else:
            r = g.value_load(k.rolev[0:1, 0:1])
            with g.If(r):
                g.dma_start(out=t["st_sh"], in_=prev[:]).then_inc(hs.h, 16)
                g.wait_ge(hs.h, 16)
                g.dma_start(out=t["fs_sh"], in_=k.noncev[0:1, 1:2]).then_inc(hs.h, 16)
                g.wait_ge(hs.h, 32)
            g.end_ifs()
            poll_flag(k, t["fs_sh"], sflag, 1, "polls")
        hr = k.ctr("hs_rd")
        hr.inc(g.dma_start(out=xfb[:, 0:NH * HD], in_=t["st_sh"]), 16)
        hr.wait(nc.vector)
        v1.inc(nc.vector.tensor_scalar(out=prev[:], in0=xfb[:, 0:NH * HD],
                                       scalar1=k.smask[:, 0:1], scalar2=None, op0=ALU.mult))

        tbuf = sb("tbuf", [128, 32 * 512], BF16)
        ytm = sb("ytm", [128, DI])
        tg = sb("tg", [128, DI], BF16)
        zch = sb("zch", [128, DI])
        xfm2 = sb("xfm2", [128, DI])
        cfm_f = sb("cfmf", [128, DS])
        cfm_b = sb("cfmb", [128, DS], BF16)
        cfm_b2 = sb("cfmb2", [128, DS], BF16)
        prev_b = xfb_b
        cbm = sb("cbm", [128, 128], BF16)
        colbc = sb("colbc", [128, 2 * 128])
        gsc = sb("gsc", [128, 128])
        rl = sb("rl", [128, 2 * 128])
        dec = sb("dec", [128, 2 * 128], BF16)
        ntb = sb("ntb", [128, 2 * 128], BF16)
        tsq = sb("tsq", [128, 128], BF16)
        y1 = sb("y1", [128, 2 * 64])
        rb2 = sb("rb2", [128, 128])
        rrow = sb("rrow", [1, 128])
        wo = sb("wo", [128, RINGO * DI], BF16)
        xpt = sb("xpt", [128, 2 * 512])
        x2st = sb("x2st", [128, 2 * 512])

        ld2 = k.ctr("p2_ld")
        t2 = k.ctr("p2_t")
        v2 = k.vctr()
        s2 = k.sctr()
        gcb = k.gctr()
        tbc = k.ctr("p2_tbc")
        tmm3 = k.ctr("p2_m3")
        vhd = k.vctr()
        syd = k.sctr()
        srl = k.sctr()
        sdc = k.sctr()
        vnt = k.vctr()
        vxw = k.vctr()
        wol = DmaRing(k, "p2_wol", RINGO)
        top = k.ctr("p2_top")
        vop = k.vctr()
        wol_mark = {}
        vop_at = []
        xpl = DmaRing(k, "p2_xpl", 2)
        x2w = DmaRing(k, "p2_x2w", 2)
        dbgw = k.ctr("p2_dbgw") if DEBUG_DUMP[0] else None

        with ExitStack() as ps2:
            kq = K(nc, ps2)
            p_bc = [kq.ps(f"p2_bc{i}", [128, 128]) for i in range(2)]
            p_hd = [kq.ps(f"p2_hd{i}", [128, 192]) for i in range(2)]
            p_tf = [kq.ps(f"p2_tf{i}", [128, 128]) for i in range(2)]
            p_op = [kq.ps(f"p2_op{i}", [128, 512]) for i in range(2)]
            p_cb = p_tf[0]
            p_rm = p_bc[0][0:1, 0:128]

            for c in range(NCK):
                b0 = HALO + c * CHK
                sweep = c // 4
                if c == 0:
                    v1.wait(nc.sync)
                    t1.wait(nc.sync)
                    w1.wait(nc.sync)
                if c > 0:
                    for eng in (nc.sync,):
                        for cc_ in (vhd, v2, t2, s2):
                            cc_.wait(eng)
                    if dbgw is not None:
                        dbgw.wait(nc.vector)
                ld2.inc(nc.sync.dma_start(out=xtm[:], in_=t["xtm_s"][c]), 16)
                ld2.inc(nc.sync.dma_start(out=btm[:], in_=t["btm_s"][c]), 16)
                ld2.inc(nc.sync.dma_start(out=cfm_f[:],
                                          in_=t["xbc_s"][DI + DS:XBCD, b0:b0 + 128]), 16)
                ld2.inc(nc.sync.dma_start(out=bfb[:],
                                          in_=t["xbc_s"][DI:DI + DS, b0:b0 + 128]), 16)
                ld2.inc(nc.sync.dma_start(
                    out=xfm2[:].rearrange("p (kt n) -> p kt n", n=128),
                    in_=t["xbc_s"][0:DI, b0:b0 + 128].rearrange("(kt p) n -> p kt n", p=128)),
                    16)
                ld2.inc(nc.sync.dma_start(
                    out=zch[:].rearrange("p (kt n) -> p kt n", n=128),
                    in_=t["zs_s"][0:DI, b0:b0 + 128].rearrange("(kt p) n -> p kt n", p=128)),
                    16)
                ld2.wait(nc.vector)
                v2.inc(nc.vector.tensor_copy(out=bfb_b[:], in_=bfb[:]))
                v2.inc(nc.vector.tensor_copy(out=cfm_b[:], in_=cfm_f[:]))
                nc.tensor.wait_ge(v2.h, v2.n)
                if c > 0:
                    for cc_ in (vhd, syd, srl, sdc, vnt, vxw, s2):
                        cc_.wait(nc.tensor)
                t2.inc(nc.tensor.matmul(p_cb[:, :], bfb_b[:], cfm_b[:], start=True, stop=True))
                nc.vector.wait_ge(t2.h, t2.n)
                v2.inc(nc.vector.tensor_tensor(out=cbm[:, :], in0=p_cb[:, :], in1=k.utm[:, :],
                                               op=ALU.mult))
                if c > 0:
                    for eng in (nc.vector, nc.scalar, nc.gpsimd):
                        for cc_ in (tmm3, tbc, syd, srl, sdc, vnt, vxw, vhd):
                            cc_.wait(eng)
                if c == 0:
                    t1.wait(nc.vector)
                v2.inc(nc.vector.tensor_copy(out=prev_b[:], in_=prev[:]))
                v2.inc(nc.vector.tensor_copy(out=cfm_b2[:], in_=cfm_f[:]))
                srl_at, syd_at, vnt_at, vhdy_at, vhdp_at = [], [], [], [], []
                for h in range(NH):
                    par = h % 2
                    if h >= 2:
                        nc.gpsimd.wait_ge(tbc.h, tbc.n - 1)
                    gcb.inc(nc.gpsimd.tensor_scalar(
                        out=colbc[:, par * 128:(par + 1) * 128], in0=k.ones2d[:, :],
                        scalar1=k.cum_tm[:, c * 64 + h:c * 64 + h + 1], scalar2=None,
                        op0=ALU.mult))
                    nc.tensor.wait_ge(gcb.h, gcb.n)
                    if h >= 2:
                        nc.tensor.wait_ge(k.sch.h, srl_at[h - 2])
                    tbc.inc(nc.tensor.transpose(p_bc[par][:, :],
                                                colbc[:, par * 128:(par + 1) * 128],
                                                k.identf[:, :]))
                    nc.scalar.wait_ge(tbc.h, tbc.n)
                    if h >= 2:
                        nc.scalar.wait_ge(k.vch.h, vnt_at[h - 2])
                        nc.scalar.wait_ge(k.vch.h, vhdy_at[h - 2])
                    srl.inc(nc.scalar.activation(out=rl[:, par * 128:(par + 1) * 128],
                                                 in_=p_bc[par][:, :], func=AF.Relu, scale=-1.0,
                                                 bias=k.cum_tm[:, c * 64 + h:c * 64 + h + 1]))
                    srl_at.append(srl.n)
                    sdc.inc(nc.scalar.activation(out=dec[:, par * 128:(par + 1) * 128],
                                                 in_=rl[:, par * 128:(par + 1) * 128],
                                                 func=AF.Exp, scale=-1.0,
                                                 bias=k.bias_tm[:, c * 64 + h:c * 64 + h + 1]))
                    nc.vector.wait_ge(sdc.h, sdc.n)
                    if h >= 2:
                        nc.vector.wait_ge(tmm3.h, tmm3.n - 1)
                    vnt.inc(nc.vector.tensor_tensor(out=ntb[:, par * 128:(par + 1) * 128],
                                                    in0=cbm[:, :],
                                                    in1=dec[:, par * 128:(par + 1) * 128],
                                                    op=ALU.mult))
                    vnt_at.append(vnt.n)
                    vxw.inc(nc.vector.tensor_scalar(
                        out=xw[:, par * HD:(par + 1) * HD], in0=xtm[:, h * HD:(h + 1) * HD],
                        scalar1=k.w_tm[:, c * 64 + h:c * 64 + h + 1], scalar2=None,
                        op0=ALU.mult))
                    nc.tensor.wait_ge(vnt.h, vnt.n)
                    nc.tensor.wait_ge(vxw.h, vxw.n)
                    if h >= 2:
                        nc.tensor.wait_ge(k.sch.h, syd_at[h - 2])
                        nc.tensor.wait_ge(k.vch.h, vhdp_at[h - 2])
                    nc.tensor.matmul(p_hd[par][:, 0:64], ntb[:, par * 128:(par + 1) * 128],
                                     xtm[:, h * HD:(h + 1) * HD], start=True, stop=True)
                    nc.tensor.matmul(p_hd[par][:, 64:128], cfm_b2[:],
                                     prev_b[:, h * HD:(h + 1) * HD], start=True, stop=True)
                    tmm3.inc(nc.tensor.matmul(p_hd[par][:, 128:192], btm[:, :],
                                              xw[:, par * HD:(par + 1) * HD],
                                              start=True, stop=True))
                    nc.scalar.wait_ge(tmm3.h, tmm3.n)
                    syd.inc(nc.scalar.copy(out=y1[:, par * 64:(par + 1) * 64],
                                           in_=p_hd[par][:, 0:64]))
                    syd_at.append(syd.n)
                    nc.vector.wait_ge(syd.h, syd.n)
                    vhd.inc(nc.vector.scalar_tensor_tensor(
                        out=ytm[:, h * HD:(h + 1) * HD], in0=p_hd[par][:, 64:128],
                        scalar=k.e_tm[:, c * 64 + h:c * 64 + h + 1],
                        in1=y1[:, par * 64:(par + 1) * 64], op0=ALU.mult, op1=ALU.add))
                    vhdy_at.append(vhd.n)
                    vhd.inc(nc.vector.scalar_tensor_tensor(
                        out=prev[:, h * HD:(h + 1) * HD], in0=prev[:, h * HD:(h + 1) * HD],
                        scalar=k.atot[:, c * 64 + h:c * 64 + h + 1],
                        in1=p_hd[par][:, 128:192], op0=ALU.mult, op1=ALU.add))
                    vhdp_at.append(vhd.n)

                if dbgw is not None:
                    vhd.wait(nc.sync)
                    dbgw.inc(nc.sync.dma_start(out=t["ydbg_s"][c], in_=ytm[:]), 16)

                if c % 4 == 0 and c > 0:
                    nc.vector.wait_ge(top.h, top.n)
                for kt in range(32):
                    par = kt % 2
                    nc.tensor.wait_ge(vhd.h, vhd.n)
                    if kt >= 2:
                        nc.tensor.wait_ge(v2.h, v2.n)
                    t2.inc(nc.tensor.transpose(p_tf[par][:, :], ytm[:, kt * 128:(kt + 1) * 128],
                                               k.identf[:, :]))
                    nc.vector.wait_ge(t2.h, t2.n)
                    v2.inc(nc.vector.scalar_tensor_tensor(
                        out=gsc[:, :], in0=xfm2[:, kt * 128:(kt + 1) * 128],
                        scalar=k.dexp[:, kt:kt + 1], in1=p_tf[par][:, :],
                        op0=ALU.mult, op1=ALU.add))
                    v2.inc(nc.vector.tensor_tensor(out=tg[:, kt * 128:(kt + 1) * 128],
                                                   in0=gsc[:, :],
                                                   in1=zch[:, kt * 128:(kt + 1) * 128],
                                                   op=ALU.mult))
                    v2.inc(nc.vector.tensor_tensor(out=tsq[:, :],
                                                   in0=tg[:, kt * 128:(kt + 1) * 128],
                                                   in1=tg[:, kt * 128:(kt + 1) * 128],
                                                   op=ALU.mult))
                    nc.tensor.wait_ge(v2.h, v2.n)
                    t2.inc(nc.tensor.matmul(p_rm, k.onecb[:, 0:1], tsq[:, :],
                                            start=(kt == 0), stop=(kt == 31)))
                nc.scalar.wait_ge(t2.h, t2.n)
                s2.inc(nc.scalar.activation(out=rrow[0:1, :], in_=p_rm, func=AF.Ln,
                                            scale=1.0 / DI, bias=k.cbias[0:1, 0:1]))
                s2.inc(nc.scalar.activation(out=rrow[0:1, :], in_=rrow[0:1, :], func=AF.Exp,
                                            scale=-0.5))
                nc.tensor.wait_ge(s2.h, s2.n)
                t2.inc(nc.tensor.matmul(p_tf[0][:, :], k.onesr[0:1, :], rrow[0:1, :],
                                        start=True, stop=True))
                nc.vector.wait_ge(t2.h, t2.n)
                v2.inc(nc.vector.tensor_copy(out=rb2[:, :], in_=p_tf[0][:, :]))
                for kt in range(32):
                    v2.inc(nc.vector.scalar_tensor_tensor(
                        out=tbuf[:, kt * 512 + (c % 4) * 128: kt * 512 + (c % 4) * 128 + 128],
                        in0=tg[:, kt * 128:(kt + 1) * 128], scalar=k.nw[:, kt:kt + 1],
                        in1=rb2[:, :], op0=ALU.mult, op1=ALU.mult))

                if dbgw is not None and c % 4 == 3:
                    v2.wait(nc.sync)
                    dbgw.inc(nc.sync.dma_start(out=t["tdbg_s"][sweep], in_=tbuf[:]), 16)

                if c % 4 == 3:
                    for mb in range(16):
                        u = sweep * 16 + mb
                        if u >= RINGO:
                            nc.gpsimd.wait_ge(top.h, u - RINGO + 1)
                        wol.issue(nc.gpsimd.dma_start(
                            out=wo[:, (u % RINGO) * DI:(u % RINGO + 1) * DI],
                            in_=t["woutw"][mb]), u % RINGO)
                        wol_mark[u] = wol.s[u % RINGO].n
                    oc0 = sweep * 512
                    nc.tensor.wait_ge(v2.h, v2.n)
                    for mb in range(16):
                        u = sweep * 16 + mb
                        opar = mb % 2
                        wol.wait(nc.tensor, u % RINGO, wol_mark[u])
                        if u >= 2:
                            nc.sync.wait_ge(k.vch.h, vop_at[u - 2])
                        xpl.issue(nc.sync.dma_start(
                            out=xpt[:, opar * 512:(opar + 1) * 512],
                            in_=t["xl_s"][mb * 128:(mb + 1) * 128, HALO + oc0:HALO + oc0 + 512]),
                            opar)
                        if u >= 2:
                            nc.tensor.wait_ge(k.vch.h, vop_at[u - 2])
                        for kt in range(32):
                            mm = nc.tensor.matmul(
                                p_op[opar][:, :],
                                wo[:, (u % RINGO) * DI + kt * 128:(u % RINGO) * DI + (kt + 1) * 128],
                                tbuf[:, kt * 512:(kt + 1) * 512],
                                start=(kt == 0), stop=(kt == 31))
                        top.inc(mm)
                        nc.vector.wait_ge(top.h, top.n)
                        xpl.wait(nc.vector, opar)
                        if u >= 2:
                            x2w.wait(nc.vector, opar)
                        vop.inc(nc.vector.tensor_tensor(
                            out=x2st[:, opar * 512:(opar + 1) * 512],
                            in0=xpt[:, opar * 512:(opar + 1) * 512],
                            in1=p_op[opar][:, :], op=ALU.add))
                        vop_at.append(vop.n)
                        nc.sync.wait_ge(vop.h, vop.n)
                        x2w.issue(nc.sync.dma_start(
                            out=t["x2_s"][mb * 128:(mb + 1) * 128, oc0:oc0 + 512],
                            in_=x2st[:, opar * 512:(opar + 1) * 512]), opar)
        barrier(k, "p12", [t2, v2, s2, vhd, x2w.s[0], x2w.s[1], top, vop, w1, hr, gcb,
                           tmm3] + ([dbgw] if dbgw is not None else []))



_NC_CACHE = {}


def _host_prep(inputs):
    import ml_dtypes
    f32 = np.float32
    u = np.asarray(inputs["L0_attn_u"], f32)
    W_in = np.asarray(inputs["L0_attn_W_in"], f32)
    conv_w = np.asarray(inputs["L0_attn_conv_w"], f32)
    conv_b = np.asarray(inputs["L0_attn_conv_b"], f32)
    dt_bias = np.asarray(inputs["L0_attn_dt_bias"], f32)
    A_log = np.asarray(inputs["L0_attn_A_log"], f32)
    Dp = np.asarray(inputs["L0_attn_D"], f32)
    norm_w = np.asarray(inputs["L0_attn_norm_w"], f32)
    W_out = np.asarray(inputs["L0_attn_W_out"], f32)
    gate_w = np.asarray(inputs["L0_ffn_gate_w"], f32)
    mask_up = np.asarray(inputs["L0_ffn_mask_up_w"], f32)
    gproj = np.asarray(inputs["L0_ffn_gate_proj_w"], f32)
    uproj = np.asarray(inputs["L0_ffn_up_proj_w"], f32)
    dproj = np.asarray(inputs["L0_ffn_down_proj_w"], f32)
    inv = np.asarray(inputs["L0_ffn_inv_map"]).astype(np.int32)

    u_fm = np.ascontiguousarray(u.reshape(4 * 4096, D).T)

    def tile_gu(w):
        wt = w.transpose(0, 2, 1)
        return np.ascontiguousarray(
            wt.reshape(G, 16, 128, 11, 128).transpose(0, 3, 2, 1, 4).reshape(G, 11, 128, 2048))

    guw = np.ascontiguousarray(np.stack([tile_gu(gproj), tile_gu(uproj)], axis=2))
    dT = dproj.transpose(0, 2, 1)
    dnw = np.ascontiguousarray(
        dT.reshape(G, 11, 128, 16, 128).transpose(0, 3, 2, 1, 4).reshape(G, 16, 128, IFF))
    W_inp = np.zeros((D, DINP), f32)
    W_inp[:, :DIN] = W_in
    winw = np.ascontiguousarray(
        W_inp.reshape(16, 128, NM_IP, 128).transpose(2, 1, 0, 3).reshape(NM_IP, 128, 2048))
    woutw = np.ascontiguousarray(
        W_out.reshape(32, 128, 16, 128).transpose(2, 1, 0, 3).reshape(16, 128, DI))
    gatet = np.ascontiguousarray(
        gate_w.T.reshape(16, 128, E).transpose(1, 0, 2).reshape(128, 16 * E))
    mgt = np.ascontiguousarray(
        mask_up.reshape(G, 16, 128, GS).transpose(2, 0, 1, 3).reshape(128, G * 16 * GS))
    convw = np.ascontiguousarray(conv_w.reshape(34, 128, 4).transpose(1, 0, 2).reshape(128, 136))
    convb = np.ascontiguousarray(conv_b.reshape(34, 128).T)
    dexp = np.ascontiguousarray(np.repeat(Dp, HD).reshape(32, 128).T)
    nw = np.ascontiguousarray(norm_w.reshape(32, 128).T)

    bdiag = np.zeros((64, 8), f32)
    for f in range(64):
        bdiag[f, f // 8] = 1.0
    bdiagT = np.ascontiguousarray(bdiag.T)
    sel8 = np.zeros((8, G * 128), f32)
    for g in range(G):
        sel8[g, g * 128:(g + 1) * 128] = 0.5

    common = dict(
        guw=guw, dnw=dnw, winw=winw, woutw=woutw, gatet=gatet, mgt=mgt,
        convw=convw, convb=convb,
        dtb=dt_bias.reshape(NH, 1).copy(), alog=A_log.reshape(NH, 1).copy(),
        dexp=dexp, nw=nw, drow=Dp.reshape(1, NH).copy(),
        invm=inv.reshape(1, 64).copy(),
        identf=np.eye(128, dtype=f32),
        identb=np.eye(128, dtype=f32).astype(ml_dtypes.bfloat16),
        utm=(np.arange(128)[:, None] <= np.arange(128)[None, :]).astype(f32),
        ones2d=np.ones((128, 128), f32),
        onesr=np.ones((1, 128), f32),
        onecb=np.ones((128, 1), f32).astype(ml_dtypes.bfloat16),
        bdiag=bdiag, bdiagT=bdiagT, sel8=sel8,
    )
    return u_fm, common


def kernel(**inputs):
    u_fm, common = _host_prep(inputs)
    rng = np.random.default_rng()
    nonce = rng.integers(1 << 20, (1 << 30), size=2, dtype=np.int64).astype(np.int32)

    if "all" not in _NC_CACHE:
        _NC_CACHE["all"] = build("all")
    nc = _NC_CACHE["all"]

    in_maps = []
    for c in range(8):
        b, r = c // 2, c % 2
        m = dict(common)
        m["u"] = np.ascontiguousarray(u_fm[:, b * 4096 + r * TOK: b * 4096 + (r + 1) * TOK])
        m["role"] = np.array([[1 - r]], np.int32)
        m["rolei"] = np.array([[r]], np.int32)
        m["nonce"] = nonce.reshape(1, 2).copy()
        m["smask"] = np.full((128, 1), float(r), np.float32)
        in_maps.append(m)

    res = run_bass_kernel_spmd(nc, in_maps, core_ids=list(range(8)))
    cols = [res.results[c]["out"] for c in range(8)]
    full = np.concatenate(cols, axis=1)
    return np.ascontiguousarray(full.T).reshape(4, 4096, D).astype(np.float32)


# version 05 (final)
# speedup vs baseline: 1.0394x; 1.0394x over previous
import numpy as np
import concourse.bass as bass
import concourse.mybir as mybir
from concourse.bass_utils import run_bass_kernel_spmd
from contextlib import ExitStack

F32 = mybir.dt.float32
BF16 = mybir.dt.bfloat16
I32 = mybir.dt.int32
AF = mybir.ActivationFunctionType
ALU = mybir.AluOpType
AX = mybir.AxisListType

D = 2048
DI = 4096
NH, HD, DS = 64, 64, 128
XBCD = DI + 2 * DS
DIN = 2 * DI + 2 * DS + NH
DINP = 8576
NM_IP = DINP // 128
IFF = 1408
G, GS, E = 8, 8, 64
TOK = 2048
HALO = 3
TOKH = TOK + HALO
CHK = 128
NCK = TOK // CHK
TH = 1024
EPS = 1e-5

DEBUG_DUMP = [False]
RACE_AUDIT = [False]
RINGP = 2
RINGD = 2
RINGW = 4
RINGO = 2

BUILD_PHASES = "all"


class Ctr:
    def __init__(self, nc, ctx, name):
        self.h = ctx.enter_context(nc.semaphore(name))
        self.n = 0

    def inc(self, inst, k=1):
        inst.then_inc(self.h, k)
        self.n += k
        return inst

    def wait(self, eng, val=None):
        eng.wait_ge(self.h, self.n if val is None else val)


class Chain:
    def __init__(self, nc, ctx, name):
        self.h = ctx.enter_context(nc.semaphore(name))
        self.n = 0

    def op(self, inst):
        inst._wait_ge(self.h, self.n)
        inst.then_inc(self.h, 1)
        self.n += 1
        return self.n


class Ctr2:
    def __init__(self, chain):
        self.chain = chain
        self.h = chain.h
        self.n = 0

    def inc(self, inst, k=1):
        self.n = self.chain.op(inst)
        return inst

    def wait(self, eng, val=None):
        eng.wait_ge(self.h, self.n if val is None else val)


class DmaRing:
    def __init__(self, k, name, nslots):
        self.s = [k.ctr(f"{name}{i}") for i in range(nslots)]

    def issue(self, inst, slot):
        return self.s[slot].inc(inst, 16)

    def wait(self, eng, slot, val=None):
        self.s[slot].wait(eng, val)


class K:
    def __init__(self, nc, ctx):
        self.nc, self.ctx = nc, ctx

    def sb(self, name, shape, dt=F32):
        return self.ctx.enter_context(self.nc.sbuf_tensor(name, shape, dt))

    def ps(self, name, shape, dt=F32):
        return self.ctx.enter_context(self.nc.psum_tensor(name, shape, dt))

    def ctr(self, name):
        return Ctr(self.nc, _ROOT_CTX[0], name)

    def vctr(self, name=None):
        return Ctr2(self.vch)

    def sctr(self, name=None):
        return Ctr2(self.sch)

    def gctr(self, name=None):
        return Ctr2(self.gch)


_ROOT_CTX = [None]


def build(phases=BUILD_PHASES, detect_races=True):
    nc = bass.Bass("TRN2", target_bir_lowering=False, num_devices=8,
                   detect_race_conditions=detect_races)
    ctx = ExitStack()
    _ROOT_CTX[0] = ctx

    def din(name, shape, dt=F32):
        return nc.dram_tensor(name, shape, dt, kind="ExternalInput").ap()

    t = {}
    t["u"] = din("u", [D, TOK])
    t["guw"] = din("guw", [G, 11, 2, 128, 2048])
    t["dnw"] = din("dnw", [G, 16, 128, IFF])
    t["winw"] = din("winw", [NM_IP, 128, 2048])
    t["woutw"] = din("woutw", [16, 128, DI])
    t["gatet"] = din("gatet", [128, 16 * E])
    t["mgt"] = din("mgt", [128, G * 16 * GS])
    t["convw"] = din("convw", [128, 34 * 4])
    t["convb"] = din("convb", [128, 34])
    t["dtb"] = din("dtb", [NH, 1])
    t["alog"] = din("alog", [NH, 1])
    t["dexp"] = din("dexp", [128, 32])
    t["nw"] = din("nw", [128, 32])
    t["drow"] = din("drow", [1, NH])
    t["invm"] = din("invm", [1, 64], I32)
    t["identf"] = din("identf", [128, 128])
    t["identb"] = din("identb", [128, 128], BF16)
    t["utm"] = din("utm", [128, 128])
    t["ones2d"] = din("ones2d", [128, 128])
    t["onesr"] = din("onesr", [1, 128])
    t["onecb"] = din("onecb", [128, 1], BF16)
    t["bdiag"] = din("bdiag", [64, 8])
    t["bdiagT"] = din("bdiagT", [8, 64])
    t["sel8"] = din("sel8", [8, G * 128])
    t["role"] = din("role", [1, 1], I32)
    t["rolei"] = din("rolei", [1, 1], I32)
    t["nonce"] = din("nonce", [1, 2], I32)
    t["smask"] = din("smask", [128, 1])

    t["out"] = nc.dram_tensor("out", [D, TOK], F32, kind="ExternalOutput").ap()

    def dsc(name, shape, dt=F32, shared=False):
        return nc.dram_tensor(name, shape, dt, kind="Internal",
                              addr_space="Shared" if shared else "Local").ap()

    shr = not RACE_AUDIT[0]
    t["xp_sh"] = dsc("xp_sh", [D, 2 * TOK], shared=shr)
    t["st_sh"] = dsc("st_sh", [DS, NH * HD], shared=shr)
    t["fx_sh"] = dsc("fx_sh", [1, 1], I32, shared=shr)
    t["fs_sh"] = dsc("fs_sh", [1, 1], I32, shared=shr)
    t["xa_s"] = dsc("xa_s", [D, TOK])
    t["xa2_s"] = dsc("xa2_s", [D, TOK])
    t["xl_s"] = dsc("xl_s", [D, TOKH])
    t["xbc_s"] = dsc("xbc_s", [XBCD, TOKH])
    t["zs_s"] = dsc("zs_s", [DI, TOKH])
    t["xtm_s"] = dsc("xtm_s", [NCK, 128, DI], BF16)
    t["btm_s"] = dsc("btm_s", [NCK, 128, DS], BF16)
    t["x2_s"] = dsc("x2_s", [D, TOK])
    t["pq_s"] = dsc("pq_s", [G, 8, 2816], BF16)
    t["sf_d"] = dsc("sf_d", [64, TH], BF16)
    if DEBUG_DUMP[0]:
        t["ydbg_s"] = dsc("ydbg_s", [NCK, 128, DI])
        t["tdbg_s"] = dsc("tdbg_s", [4, 128, 32 * 512], BF16)
        t["pdbg_s"] = dsc("pdbg_s", [NCK, 128, NH * HD])

    with nc.Block():
        k = K(nc, ctx)
        emit(k, t, phases)
    ctx.close()
    return nc


def barrier(k, name, ctrs=()):
    nc = k.nc
    b = k.ctr("bar_" + name)
    engs = [nc.tensor, nc.vector, nc.scalar, nc.gpsimd, nc.sync]
    for e in engs:
        for c in ctrs:
            c.wait(e)
        e.nop().then_inc(b.h, 1)
    b.n = 5
    for e in engs:
        b.wait(e)


def emit(k, t, phases):
    nc = k.nc
    k.vch = Chain(nc, _ROOT_CTX[0], "chainV")
    k.sch = Chain(nc, _ROOT_CTX[0], "chainS")
    k.gch = Chain(nc, _ROOT_CTX[0], "chainG")

    k.identf = k.sb("c_identf", [128, 128])
    k.identb = k.sb("c_identb", [128, 128], BF16)
    k.utm = k.sb("c_utm", [128, 128])
    k.ones2d = k.sb("c_ones2d", [128, 128])
    k.onesr = k.sb("c_onesr", [1, 128])
    k.onecb = k.sb("c_onecb", [128, 1], BF16)
    k.bdiag = k.sb("c_bdiag", [64, 8])
    k.bdiagT = k.sb("c_bdiagT", [8, 64])
    k.sel8 = k.sb("c_sel8", [8, G * 128])
    k.gatet = k.sb("c_gatet", [128, 16 * E], BF16)
    k.mgt = k.sb("c_mgt", [128, G * 16 * GS], BF16)
    k.convw = k.sb("c_convw", [128, 34 * 4])
    k.convb = k.sb("c_convb", [128, 34])
    k.dtb = k.sb("c_dtb", [NH, 1])
    k.nA = k.sb("c_nA", [NH, 1])
    k.dexp = k.sb("c_dexp", [128, 32])
    k.nw = k.sb("c_nw", [128, 32])
    k.drow = k.sb("c_drow", [1, NH])
    k.invr = k.sb("c_invr", [1, 64], I32)
    k.invf = k.sb("c_invf", [1, 64])
    k.iota = k.sb("c_iota", [64, 1])
    k.pmat = k.sb("c_pmat", [64, 64])
    k.rolev = k.sb("c_role", [1, 1], I32)
    k.roleiv = k.sb("c_rolei", [1, 1], I32)
    k.noncev = k.sb("c_nonce", [1, 2], I32)
    k.smask = k.sb("c_smask", [128, 1])
    k.zero3 = k.sb("c_zero3", [128, 48])
    k.cbias = k.sb("c_cbias", [128, 2])
    k.dbc = k.sb("c_dbc", [128, NH])
    alog_t = k.sb("c_alog", [NH, 1])

    cl = k.ctr("c_ld")
    for dst, src in ((k.identf, t["identf"]), (k.utm, t["utm"]),
                     (k.ones2d, t["ones2d"]), (k.onesr, t["onesr"]),
                     (k.bdiag, t["bdiag"]), (k.bdiagT, t["bdiagT"]),
                     (k.sel8, t["sel8"]), (k.convw, t["convw"]),
                     (k.convb, t["convb"]), (k.dtb, t["dtb"]),
                     (alog_t, t["alog"]), (k.dexp, t["dexp"]),
                     (k.nw, t["nw"]), (k.drow, t["drow"]),
                     (k.invr, t["invm"]), (k.rolev, t["role"]),
                     (k.roleiv, t["rolei"]), (k.noncev, t["nonce"]),
                     (k.smask, t["smask"]), (k.identb, t["identb"]),
                     (k.onecb, t["onecb"])):
        cl.inc(nc.sync.dma_start(out=dst[:], in_=src), 16)
    clg = k.ctr("c_ldg")
    clg.inc(nc.gpsimd.dma_start(out=k.gatet[:], in_=t["gatet"]), 16)
    clg.inc(nc.gpsimd.dma_start(out=k.mgt[:], in_=t["mgt"]), 16)

    cd = k.vctr()
    cs = k.sctr()
    tin = k.ctr("c_tin")
    pictx = ExitStack()
    pinit = K(nc, pictx).ps("p_init", [128, 128])
    cl.wait(nc.vector)
    cd.inc(nc.vector.memset(k.zero3[:], 0.0))
    cd.inc(nc.vector.memset(k.cbias[:, 0:1], EPS))
    cd.inc(nc.vector.memset(k.cbias[:, 1:2], 1.0))
    cd.inc(nc.vector.tensor_copy(out=k.invf[:], in_=k.invr[:]))
    cd.inc(nc.vector.reduce_sum(out=k.iota[0:64, 0:1], in_=k.utm[0:64, :], axis=AX.X))
    cd.inc(nc.vector.tensor_scalar(out=k.iota[0:64, 0:1], in0=k.iota[0:64, 0:1],
                                   scalar1=-1.0, scalar2=128.0, op0=ALU.mult, op1=ALU.add))
    cl.wait(nc.scalar)
    cs.inc(nc.scalar.activation(out=k.nA[:], in_=alog_t[:], func=AF.Exp))
    cs.wait(nc.vector)
    cd.inc(nc.vector.tensor_scalar(out=k.nA[:], in0=k.nA[:], scalar1=-1.0,
                                   scalar2=None, op0=ALU.mult))
    cd.wait(nc.tensor)
    tin.inc(nc.tensor.matmul(pinit[0:64, 0:64], k.onesr[0:1, 0:64], k.invf[0:1, :],
                             start=True, stop=True))
    tin.wait(nc.vector)
    cd.inc(nc.vector.tensor_scalar(out=k.pmat[:], in0=pinit[0:64, 0:64],
                                   scalar1=k.iota[0:64, 0:1], scalar2=None,
                                   op0=ALU.is_equal))
    cd.wait(nc.tensor)
    tin.inc(nc.tensor.matmul(pinit[:, 0:NH], k.onesr[0:1, :], k.drow[0:1, :],
                             start=True, stop=True))
    tin.wait(nc.vector)
    cd.inc(nc.vector.tensor_copy(out=k.dbc[:], in_=pinit[:, 0:NH]))

    pictx.close()
    barrier(k, "init", [cl, clg, cd, cs, tin])

    if phases == "ffn1":
        emit_ffn(k, t, call=0, src=t["u"], acc=t["xa2_s"])
        fin = k.ctr("fincp")
        fin.inc(nc.sync.dma_start(out=t["out"], in_=t["xa2_s"]), 16)
        fin.wait(nc.sync)
        barrier(k, "fin")
        return
    emit_ffn(k, t, call=0, src=t["u"], acc=t["xa_s"])
    emit_handoff_x(k, t)
    barrier(k, "xl")
    with ExitStack() as ms:
        km = K(nc, ms)
        k.dtc_tm = km.sb("m_dtctm", [128, NCK * 64])
        k.cum_tm = km.sb("m_cumtm", [128, NCK * 64])
        k.e_tm = km.sb("m_etm", [128, NCK * 64])
        k.bias_tm = km.sb("m_biastm", [128, NCK * 64])
        k.w_tm = km.sb("m_wtm", [128, NCK * 64])
        k.atot = km.sb("m_atot", [128, NCK * 64])
        with ExitStack() as msf:
            kf = K(nc, msf)
            emit_inproj(k, t, kf)
            barrier(k, "ip")
            if phases == "inproj":
                return
            emit_dt(k, t, km)
            barrier(k, "dt")
        emit_pass12(k, t, km)
        barrier(k, "p2")
    if phases == "mamba":
        return
    emit_ffn(k, t, call=1, src=t["x2_s"], acc=t["xa2_s"])
    fin = k.ctr("fincp")
    fin.inc(nc.sync.dma_start(out=t["out"], in_=t["xa2_s"]), 16)
    fin.wait(nc.sync)
    barrier(k, "end")



FFN_LIMIT = ["full"]


def emit_ffn(k, t, call, src, acc):
    nc = k.nc
    with ExitStack() as fs:
        kk = K(nc, fs)
        pre = f"f{call}_"
        sb = lambda n, s, d=F32: kk.sb(pre + n, s, d)
        xtl = sb("xtl", [128, 2 * TH])
        xln = sb("xln", [128, 16 * TH], BF16)
        wgu = sb("wgu", [128, 2 * RINGP * 2048], BF16)
        wdn = sb("wdn", [128, RINGD * IFF], BF16)
        abuf = sb("abuf", [128, 2 * 11 * TH], BF16)
        pqb = sb("pqb", [8, 2 * 2816], BF16)
        sbg = sb("sbg", [128, 2 * TH], BF16)
        tmp1 = sb("tmp1", [128, 2 * 512])
        tmp2 = sb("tmp2", [128, 2 * 512])
        xbr = sb("xbr", [128, TH], BF16)
        rbm = sb("rbm", [128, 2 * TH])
        srow = sb("srow", [1, 2 * TH])
        lgs = sb("lgs", [64, 512])
        lgt = sb("lgt", [128, 4 * 64])
        m8 = sb("m8", [128, 8])
        rep = sb("rep", [128, 4 * 64])
        wet = sb("wet", [128, 4 * 64])
        wef = sb("wef", [64, TH])
        fl64 = sb("fl64", [64, TH])
        rcp8 = sb("rcp8", [8, TH])
        scl8 = sb("scl8", [8, TH])
        soft8 = sb("soft8", [8, G * TH], BF16)
        sm1 = sb("sm1", [128, 4])
        bnc = rbm

        xl_ld = DmaRing(k, pre + "xld", 2)
        xa_ld = DmaRing(k, pre + "xald", 2)
        xa_st = DmaRing(k, pre + "xast", 2)
        cpy = k.ctr(pre + "cpy")
        gu_ld = DmaRing(k, pre + "guld", RINGP)
        dn_ld = DmaRing(k, pre + "dnld", RINGD)
        pq_ld = DmaRing(k, pre + "pqld", 2)
        pq_st = k.ctr(pre + "pqst")
        mmg = k.ctr(pre + "mmg")
        mmu = k.ctr(pre + "mmu")
        mmd = k.ctr(pre + "mmd")
        ssil = k.sctr()
        vt2 = k.vctr()
        vab = k.vctr()
        vrm = k.vctr()
        tsb = k.ctr(pre + "tsb")
        vsb = k.vctr()
        tpq = k.ctr(pre + "tpq")
        vpq = k.vctr()

        cpy.inc(nc.sync.dma_start(out=acc, in_=src), 16)

        for half in range(2):
            first = (call == 0 and half == 0)
            c0 = half * TH
            mmu0, mmd0, mmg0 = mmu.n, mmd.n, mmg.n
            ab_done = {}

            gu_mark, dn_mark, pq_mark = {}, {}, {}
            if FFN_LIMIT[0] == "A":
                pass
            elif not first:
                pq_st.wait(nc.gpsimd)
            for p in range(4):
                if FFN_LIMIT[0] == "A":
                    break
                for g in (2 * p, 2 * p + 1):
                    if not first:
                        if g >= 2:
                            nc.gpsimd.wait_ge(mmu.h, mmu0 + 22 * (g - 1))
                        pq_ld.issue(nc.gpsimd.dma_start(
                            out=pqb[0:8, (g % 2) * 2816:(g % 2) * 2816 + 2816],
                            in_=t["pq_s"][g]), g % 2)
                        pq_mark[g] = pq_ld.s[g % 2].n
                    for mb in range(11):
                        u = g * 11 + mb
                        if u >= RINGP:
                            nc.gpsimd.wait_ge(mmu.h, mmu0 + 2 * (u - RINGP + 1))
                        sl = (u % RINGP) * 2
                        gu_ld.issue(nc.gpsimd.dma_start(
                            out=wgu[:, sl * 2048:(sl + 1) * 2048], in_=t["guw"][g, mb, 0]),
                            u % RINGP)
                        gu_ld.issue(nc.gpsimd.dma_start(
                            out=wgu[:, (sl + 1) * 2048:(sl + 2) * 2048], in_=t["guw"][g, mb, 1]),
                            u % RINGP)
                        gu_mark[u] = gu_ld.s[u % RINGP].n
                for hb in range(16):
                    if FFN_LIMIT[0] == "B1":
                        break
                    for gi in range(2):
                        u = (p * 16 + hb) * 2 + gi
                        if u >= RINGD:
                            v = u - RINGD
                            nc.gpsimd.wait_ge(mmd.h, mmd0 + 2 * (v // 2 + 1))
                        sl = u % RINGD
                        dn_ld.issue(nc.gpsimd.dma_start(
                            out=wdn[:, sl * IFF:(sl + 1) * IFF], in_=t["dnw"][2 * p + gi, hb]),
                            sl)
                        dn_mark[u] = dn_ld.s[sl].n

            vA = k.vctr()
            tA = k.ctr(f"{pre}h{half}tA")
            sA = k.sctr()

            if half == 1:
                nc.vector.wait_ge(mmg.h, mmg0)
                nc.vector.wait_ge(mmu.h, mmu0)
                nc.sync.wait_ge(mmg.h, mmg0)
                xa_st.wait(nc.vector, 0)
                xa_st.wait(nc.vector, 1)
            if call == 0 and half == 0:
                cpy.wait(nc.sync)

            xtl_free = [0, 0]
            with ExitStack() as ps1:
                kq = K(nc, ps1)
                ps1b = ExitStack()
                kqb = K(nc, ps1b)
                p_sx = kqb.ps(f"{pre}h{half}sx", [1, TH])
                p_sx2 = kqb.ps(f"{pre}h{half}sx2", [1, TH])
                for kt in range(16):
                    par = kt % 2
                    nc.sync.wait_ge(k.vch.h, xtl_free[par])
                    xl_ld.issue(nc.sync.dma_start(
                        out=xtl[:, par * TH:(par + 1) * TH],
                        in_=src[kt * 128:(kt + 1) * 128, c0:c0 + TH]), par)
                    xl_ld.wait(nc.vector, par)
                    tA.wait(nc.vector)
                    vA.inc(nc.vector.tensor_copy(out=xbr[:, :],
                                                 in_=xtl[:, par * TH:(par + 1) * TH]))
                    xtl_free[par] = vA.n
                    vA.wait(nc.tensor)
                    for n in range(2):
                        tA.inc(nc.tensor.matmul(
                            p_sx[0:1, n * 512:(n + 1) * 512], k.onecb[:, 0:1],
                            xbr[:, n * 512:(n + 1) * 512],
                            start=(kt == 0), stop=(kt == 15)))
                    tA.wait(nc.vector)
                    vA.inc(nc.vector.tensor_tensor(out=xbr[:, :], in0=xbr[:, :],
                                                   in1=xbr[:, :], op=ALU.mult))
                    vA.wait(nc.tensor)
                    for n in range(2):
                        tA.inc(nc.tensor.matmul(
                            p_sx2[0:1, n * 512:(n + 1) * 512], k.onecb[:, 0:1],
                            xbr[:, n * 512:(n + 1) * 512],
                            start=(kt == 0), stop=(kt == 15)))
                tA.wait(nc.vector)
                vA.inc(nc.vector.tensor_scalar(out=srow[0:1, 0:TH], in0=p_sx[0:1, :],
                                               scalar1=1.0 / D, scalar2=None, op0=ALU.mult))
                vA.inc(nc.vector.tensor_scalar(out=srow[0:1, TH:2 * TH], in0=p_sx2[0:1, :],
                                               scalar1=1.0 / D, scalar2=None, op0=ALU.mult))
                vA.inc(nc.vector.tensor_tensor(out=scl8[0:1, 0:TH],
                                               in0=srow[0:1, 0:TH], in1=srow[0:1, 0:TH],
                                               op=ALU.mult))
                vA.inc(nc.vector.tensor_tensor(out=srow[0:1, TH:2 * TH],
                                               in0=srow[0:1, TH:2 * TH],
                                               in1=scl8[0:1, 0:TH], op=ALU.subtract))
                vA.wait(nc.scalar)
                sA.inc(nc.scalar.activation(out=scl8[0:1, 0:TH],
                                            in_=srow[0:1, TH:2 * TH], func=AF.Ln, bias=k.cbias[0:1, 0:1]))
                sA.inc(nc.scalar.activation(out=srow[0:1, TH:2 * TH],
                                            in_=scl8[0:1, 0:TH], func=AF.Exp,
                                            scale=-0.5))
                sA.wait(nc.vector)
                vA.inc(nc.vector.tensor_tensor(out=srow[0:1, 0:TH],
                                               in0=srow[0:1, 0:TH],
                                               in1=srow[0:1, TH:2 * TH], op=ALU.mult))
                vA.inc(nc.vector.tensor_scalar(out=srow[0:1, 0:TH],
                                               in0=srow[0:1, 0:TH],
                                               scalar1=-1.0, scalar2=None, op0=ALU.mult))
                ps1b.close()
                p_rb = kq.ps(f"{pre}h{half}rb", [128, TH])
                p_mrb = kq.ps(f"{pre}h{half}mrb", [128, TH])
                vA.wait(nc.tensor)
                for n in range(2):
                    tA.inc(nc.tensor.matmul(p_rb[:, n * 512:(n + 1) * 512], k.onesr[0:1, :],
                                            srow[0:1, TH + n * 512: TH + (n + 1) * 512],
                                            start=True, stop=True))
                    tA.inc(nc.tensor.matmul(p_mrb[:, n * 512:(n + 1) * 512], k.onesr[0:1, :],
                                            srow[0:1, n * 512:(n + 1) * 512],
                                            start=True, stop=True))
                tA.wait(nc.vector)
                vA.inc(nc.vector.tensor_copy(out=rbm[:, 0:TH], in_=p_rb[:]))
                vA.inc(nc.vector.tensor_copy(out=rbm[:, TH:2 * TH], in_=p_mrb[:]))
                for kt in range(16):
                    par = kt % 2
                    nc.sync.wait_ge(k.vch.h, xtl_free[par])
                    xl_ld.issue(nc.sync.dma_start(
                        out=xtl[:, par * TH:(par + 1) * TH],
                        in_=src[kt * 128:(kt + 1) * 128, c0:c0 + TH]), par)
                    xl_ld.wait(nc.vector, par)
                    vA.inc(nc.vector.tensor_tensor(out=xtl[:, par * TH:(par + 1) * TH],
                                                   in0=xtl[:, par * TH:(par + 1) * TH],
                                                   in1=rbm[:, 0:TH], op=ALU.mult))
                    vA.inc(nc.vector.tensor_tensor(out=xln[:, kt * TH:(kt + 1) * TH],
                                                   in0=xtl[:, par * TH:(par + 1) * TH],
                                                   in1=rbm[:, TH:2 * TH], op=ALU.add))
                    xtl_free[par] = vA.n
            vln_done = vA.n

            with ExitStack() as ps3:
                kq = K(nc, ps3)
                p_lg = kq.ps(f"{pre}h{half}lg", [64, 512])
                p_tr = kq.ps(f"{pre}h{half}tr", [128, 128])
                p_g8 = kq.ps(f"{pre}h{half}g8", [8, 512])
                p_x64 = kq.ps(f"{pre}h{half}x64", [64, 512])
                for n in range(2):
                    nc.tensor.wait_ge(vA.h, vln_done)
                    if n == 1:
                        vA.wait(nc.tensor)
                    for kt in range(16):
                        tA.inc(nc.tensor.matmul(
                            p_lg[:, :], k.gatet[:, kt * 64:(kt + 1) * 64],
                            xln[:, kt * TH + n * 512: kt * TH + (n + 1) * 512],
                            start=(kt == 0), stop=(kt == 15)))
                    tA.wait(nc.vector)
                    vA.inc(nc.vector.tensor_copy(out=lgs[:, :], in_=p_lg[:]))
                    for st in range(4):
                        vA.wait(nc.tensor)
                        tA.inc(nc.tensor.transpose(
                            p_tr[0:128, 0:64], lgs[0:64, st * 128:(st + 1) * 128],
                            k.identf[0:64, 0:64]))
                        tA.wait(nc.vector)
                        vA.inc(nc.vector.tensor_copy(out=lgt[:, st * 64:(st + 1) * 64],
                                                     in_=p_tr[0:128, 0:64]))
                    for st in range(4):
                        sl_ap = lgt[:, st * 64:(st + 1) * 64]
                        vA.inc(nc.vector.reduce_max(out=sm1[:, 0:1], in_=sl_ap, axis=AX.X,
                                                    negate=True))
                        vA.wait(nc.scalar)
                        sA.inc(nc.scalar.activation(out=rep[:, st * 64:(st + 1) * 64],
                                                    in_=sl_ap, func=AF.Exp, bias=sm1[:, 0:1]))
                        sA.wait(nc.vector)
                        ex = rep[:, st * 64:(st + 1) * 64]
                        vA.inc(nc.vector.reduce_sum(out=sm1[:, 1:2], in_=ex, axis=AX.X))
                        vA.inc(nc.vector.reciprocal(out=sm1[:, 2:3], in_=sm1[:, 1:2]))
                        vA.inc(nc.vector.tensor_scalar(out=sl_ap, in0=ex, scalar1=sm1[:, 2:3],
                                                       scalar2=None, op0=ALU.mult))
                        vA.inc(nc.vector.max(out=m8[:], in_=sl_ap))
                        vA.inc(nc.vector.match_replace(out=ex, in_to_replace=m8[:],
                                                       in_values=sl_ap, imm_value=-1e30))
                        vA.inc(nc.vector.tensor_single_scalar(
                            out=wet[:, st * 64:(st + 1) * 64], in_=ex,
                            scalar=-1e30, op=ALU.is_equal))
                        vA.inc(nc.vector.tensor_mul(out=wet[:, st * 64:(st + 1) * 64],
                                                    in0=wet[:, st * 64:(st + 1) * 64],
                                                    in1=sl_ap))
                    for st in range(4):
                        vA.wait(nc.tensor)
                        tA.inc(nc.tensor.transpose(
                            p_tr[0:64, 0:128], wet[:, st * 64:(st + 1) * 64], k.identf[:]))
                        tA.wait(nc.vector)
                        vA.inc(nc.vector.tensor_copy(
                            out=wef[0:64, n * 512 + st * 128: n * 512 + (st + 1) * 128],
                            in_=p_tr[0:64, 0:128]))
                for n in range(2):
                    vA.wait(nc.tensor)
                    tA.inc(nc.tensor.matmul(p_x64[:, :], k.pmat[:, :],
                                            wef[0:64, n * 512:(n + 1) * 512],
                                            start=True, stop=True))
                    tA.wait(nc.vector)
                    vA.inc(nc.vector.tensor_copy(out=fl64[0:64, n * 512:(n + 1) * 512],
                                                 in_=p_x64[:]))
                for n in range(2):
                    vA.wait(nc.tensor)
                    tA.inc(nc.tensor.matmul(p_g8[:, :], k.bdiag[:, :],
                                            fl64[0:64, n * 512:(n + 1) * 512],
                                            start=True, stop=True))
                    tA.wait(nc.vector)
                    vA.inc(nc.vector.tensor_copy(out=scl8[0:8, n * 512:(n + 1) * 512],
                                                 in_=p_g8[:]))
                vA.inc(nc.vector.tensor_single_scalar(out=wef[0:64, :], in_=fl64[0:64, :],
                                                      scalar=0.0, op=ALU.is_equal))
                vA.inc(nc.vector.scalar_tensor_tensor(out=fl64[0:64, :], in0=wef[0:64, :],
                                                      scalar=-1e9, in1=fl64[0:64, :],
                                                      op0=ALU.mult, op1=ALU.add))
                vA.wait(nc.scalar)
                sA.inc(nc.scalar.activation(out=fl64[0:64, :], in_=fl64[0:64, :], func=AF.Exp))
                for n in range(2):
                    sA.wait(nc.tensor)
                    vA.wait(nc.tensor)
                    tA.inc(nc.tensor.matmul(p_g8[:, :], k.bdiag[:, :],
                                            fl64[0:64, n * 512:(n + 1) * 512],
                                            start=True, stop=True))
                    tA.wait(nc.vector)
                    vA.inc(nc.vector.tensor_scalar(out=rcp8[0:8, n * 512:(n + 1) * 512],
                                                   in0=p_g8[:], scalar1=1e-30,
                                                   scalar2=None, op0=ALU.add))
                    vA.inc(nc.vector.reciprocal(out=rcp8[0:8, n * 512:(n + 1) * 512],
                                                in_=rcp8[0:8, n * 512:(n + 1) * 512]))
                    vA.wait(nc.tensor)
                    tA.inc(nc.tensor.matmul(p_x64[:, :], k.bdiagT[:, :],
                                            rcp8[0:8, n * 512:(n + 1) * 512],
                                            start=True, stop=True))
                    tA.wait(nc.vector)
                    vA.inc(nc.vector.tensor_tensor(out=xbr[0:64, n * 512:(n + 1) * 512],
                                                   in0=fl64[0:64, n * 512:(n + 1) * 512],
                                                   in1=p_x64[:], op=ALU.mult))
                vA.wait(nc.sync)
                xs_r = k.ctr(f"{pre}h{half}shf")
                xs_r.inc(nc.sync.dma_start(out=t["sf_d"], in_=xbr[0:64, :]), 16)
                xs_r.wait(nc.sync)
                xs_r.inc(nc.sync.dma_start(
                    out=soft8[0:8, :].rearrange("s (g q) -> s g q", q=TH),
                    in_=t["sf_d"].rearrange("(g s) q -> s g q", s=8)), 16)
                xs_r.wait(nc.tensor)
                xs_r.wait(nc.vector)
            stA_v, stA_s = vA.n, sA.n
            nc.sync.wait_ge(vA.h, stA_v)
            if FFN_LIMIT[0] == "A":
                continue

            with ExitStack() as psb:
                kq = K(nc, psb)
                p_g = [kq.ps(f"{pre}h{half}pg{i}", [128, 512]) for i in range(2)]
                p_u = [kq.ps(f"{pre}h{half}pu{i}", [128, 512]) for i in range(2)]
                p_d = [kq.ps(f"{pre}h{half}pd{i}", [128, 512]) for i in range(2)]
                p_s = kq.ps(f"{pre}h{half}psb", [128, 512])

                nc.tensor.wait_ge(vA.h, stA_v)
                nc.tensor.wait_ge(sA.h, stA_s)
                nc.vector.wait_ge(sA.h, stA_s)
                vt2.wait(nc.scalar)

                q = 0
                ps_free = 0
                sil_at, t2_at, rm_at = [], [], []
                for p in range(4):
                    for gi in range(2):
                        g = 2 * p + gi
                        for n in range(2):
                            nc.tensor.wait_ge(k.vch.h, ps_free)
                            tsb.inc(nc.tensor.matmul(
                                p_s[:, :], k.sel8[0:8, g * 128:(g + 1) * 128],
                                scl8[0:8, n * 512:(n + 1) * 512],
                                start=True, stop=True))
                            tsb.wait(nc.vector)
                            vsb.inc(nc.vector.tensor_copy(
                                out=sbg[:, (g % 2) * TH + n * 512:(g % 2) * TH + (n + 1) * 512],
                                in_=p_s[:]))
                            ps_free = vsb.n
                        if g >= 2 and FFN_LIMIT[0] != "B1":
                            pv = (g - 2) // 2
                            nc.vector.wait_ge(mmd.h, mmd0 + 32 * (pv + 1))
                        if not first:
                            pq_ld.wait(nc.tensor, g % 2, pq_mark[g])

                        for mb in range(11):
                            u = g * 11 + mb
                            gu_ld.wait(nc.tensor, u % RINGP, gu_mark[u])
                            sl = (u % RINGP) * 2
                            wg_ap = wgu[:, sl * 2048:(sl + 1) * 2048]
                            wu_ap = wgu[:, (sl + 1) * 2048:(sl + 2) * 2048]
                            pofP = (g % 2) * 2816 + mb * 128
                            pofQ = (g % 2) * 2816 + 1408 + mb * 128

                            if first:
                                nc.tensor.wait_ge(k.vch.h, ps_free)
                                for kt in range(16):
                                    tpq.inc(nc.tensor.matmul(
                                        p_s[0:8, 0:128],
                                        k.mgt[:, (g * 16 + kt) * 8:(g * 16 + kt + 1) * 8],
                                        wg_ap[:, kt * 128:(kt + 1) * 128],
                                        start=(kt == 0), stop=(kt == 15)))
                                tpq.wait(nc.vector)
                                vpq.inc(nc.vector.tensor_copy(
                                    out=pqb[0:8, pofP:pofP + 128], in_=p_s[0:8, 0:128]))
                                ps_free = vpq.n
                                pqP_mark = vpq.n
                                nc.tensor.wait_ge(k.vch.h, ps_free)
                                for kt in range(16):
                                    tpq.inc(nc.tensor.matmul(
                                        p_s[0:8, 0:128],
                                        k.mgt[:, (g * 16 + kt) * 8:(g * 16 + kt + 1) * 8],
                                        wu_ap[:, kt * 128:(kt + 1) * 128],
                                        start=(kt == 0), stop=(kt == 15)))
                                tpq.wait(nc.vector)
                                vpq.inc(nc.vector.tensor_copy(
                                    out=pqb[0:8, pofQ:pofQ + 128], in_=p_s[0:8, 0:128]))
                                ps_free = vpq.n
                                pqQ_mark = vpq.n

                            for n in range(2):
                                par = q % 2
                                if q >= 2:
                                    nc.tensor.wait_ge(k.sch.h, sil_at[q - 2])
                                if first:
                                    nc.tensor.wait_ge(k.vch.h, pqP_mark)
                                for kt in range(16):
                                    nc.tensor.matmul(
                                        p_g[par][:, :], wg_ap[:, kt * 128:(kt + 1) * 128],
                                        xln[:, kt * TH + n * 512: kt * TH + (n + 1) * 512],
                                        start=(kt == 0), stop=False)
                                mmg.inc(nc.tensor.matmul(
                                    p_g[par][:, :], pqb[0:8, pofP:pofP + 128],
                                    soft8[0:8, g * TH + n * 512: g * TH + (n + 1) * 512],
                                    start=False, stop=True))
                                if q >= 2:
                                    nc.tensor.wait_ge(k.vch.h, t2_at[q - 2])
                                if first:
                                    nc.tensor.wait_ge(k.vch.h, pqQ_mark)
                                for kt in range(16):
                                    nc.tensor.matmul(
                                        p_u[par][:, :], wu_ap[:, kt * 128:(kt + 1) * 128],
                                        xln[:, kt * TH + n * 512: kt * TH + (n + 1) * 512],
                                        start=(kt == 0), stop=False)
                                mmu.inc(nc.tensor.matmul(
                                    p_u[par][:, :], pqb[0:8, pofQ:pofQ + 128],
                                    soft8[0:8, g * TH + n * 512: g * TH + (n + 1) * 512],
                                    start=False, stop=True))

                                nc.scalar.wait_ge(mmg.h, mmg0 + q + 1)
                                if q >= 2:
                                    nc.scalar.wait_ge(k.vch.h, t2_at[q - 2])
                                ssil.inc(nc.scalar.activation(
                                    out=tmp1[:, par * 512:(par + 1) * 512],
                                    in_=p_g[par][:, :], func=AF.Silu))
                                sil_at.append(ssil.n)
                                ssil.wait(nc.vector)
                                nc.vector.wait_ge(mmu.h, mmu0 + q + 1)
                                vt2.inc(nc.vector.tensor_tensor(
                                    out=tmp2[:, par * 512:(par + 1) * 512],
                                    in0=tmp1[:, par * 512:(par + 1) * 512],
                                    in1=p_u[par][:, :], op=ALU.mult))
                                t2_at.append(vt2.n)
                                a0 = (g % 2) * 11 * TH + mb * TH + n * 512
                                vab.inc(nc.vector.tensor_tensor(
                                    out=abuf[:, a0:a0 + 512],
                                    in0=tmp2[:, par * 512:(par + 1) * 512],
                                    in1=sbg[:, (g % 2) * TH + n * 512:(g % 2) * TH + (n + 1) * 512],
                                    op=ALU.mult))
                                ab_done[g] = vab.n
                                q += 1
                        if first:
                            vpq.wait(nc.gpsimd)
                            pq_st.inc(nc.gpsimd.dma_start(
                                out=t["pq_s"][g],
                                in_=pqb[0:8, (g % 2) * 2816:(g % 2) * 2816 + 2816]), 16)

                    if FFN_LIMIT[0] == "B1":
                        continue
                    nc.tensor.wait_ge(vab.h, ab_done[2 * p + 1])
                    for hb in range(16):
                        for gi2 in range(2):
                            uu = (p * 16 + hb) * 2 + gi2
                            dn_ld.wait(nc.tensor, uu % RINGD, dn_mark[uu])
                        for n in range(2):
                            dpar = (hb * 2 + n) % 2
                            idx = p * 32 + hb * 2 + n
                            if FFN_LIMIT[0] != "B2":
                                if idx < 2:
                                    cpy.wait(nc.sync)
                                if idx >= 2:
                                    nc.sync.wait_ge(k.vch.h, rm_at[idx - 2])
                                if idx >= 32:
                                    xa_st.wait(nc.sync, idx % 2)
                                xa_ld.issue(nc.sync.dma_start(
                                    out=xtl[:, (idx % 2) * TH + 512:(idx % 2) * TH + 1024],
                                    in_=acc[hb * 128:(hb + 1) * 128,
                                            c0 + n * 512:c0 + (n + 1) * 512]), idx % 2)
                            if idx >= 2:
                                nc.tensor.wait_ge(k.vch.h, rm_at[idx - 2])
                            for gi2 in range(2):
                                g2 = 2 * p + gi2
                                sld = ((p * 16 + hb) * 2 + gi2) % RINGD
                                for kb in range(11):
                                    lst = (gi2 == 1 and kb == 10)
                                    a0 = (g2 % 2) * 11 * TH + kb * TH + n * 512
                                    mm = nc.tensor.matmul(
                                        p_d[dpar][:, :],
                                        wdn[:, sld * IFF + kb * 128: sld * IFF + (kb + 1) * 128],
                                        abuf[:, a0:a0 + 512],
                                        start=(gi2 == 0 and kb == 0), stop=lst)
                                    if lst:
                                        mmd.inc(mm)
                            nc.vector.wait_ge(mmd.h, mmd0 + idx + 1)
                            if FFN_LIMIT[0] == "B2":
                                vrm.inc(nc.vector.tensor_copy(
                                    out=bnc[:, (idx % 2) * 512:(idx % 2 + 1) * 512],
                                    in_=p_d[dpar][:, :]))
                                rm_at.append(vrm.n)
                                continue
                            xa_ld.wait(nc.vector, idx % 2)
                            if idx >= 2:
                                xa_st.wait(nc.vector, idx % 2)
                            vrm.inc(nc.vector.tensor_tensor(
                                out=bnc[:, (idx % 2) * 512:(idx % 2 + 1) * 512],
                                in0=xtl[:, (idx % 2) * TH + 512:(idx % 2) * TH + 1024],
                                in1=p_d[dpar][:, :], op=ALU.add))
                            rm_at.append(vrm.n)
                            vrm.wait(nc.sync)
                            xa_st.issue(nc.sync.dma_start(
                                out=acc[hb * 128:(hb + 1) * 128,
                                        c0 + n * 512:c0 + (n + 1) * 512],
                                in_=bnc[:, (idx % 2) * 512:(idx % 2 + 1) * 512]), idx % 2)

        xa_st.wait(nc.sync, 0)
        xa_st.wait(nc.sync, 1)
        barrier(k, f"ffn{call}", [xa_st.s[0], xa_st.s[1], vrm, vab, mmd, ssil, vt2, vsb, pq_st])



def poll_flag(k, flag_dram, flag_sb, nonce_col, tag):
    nc = k.nc
    g = nc.gpsimd
    r_nonce = g.value_load(k.noncev[0:1, nonce_col:nonce_col + 1])
    cnt = g.alloc_register(tag + "_cnt")
    g.reg_mov(cnt, 0)
    fr = g.alloc_register(tag + "_fr")
    pl = k.ctr(tag + "_pl")
    loop, done = tag + "_loop", tag + "_done"
    g.br(loop)
    with nc.body(loop):
        g.reg_add(cnt, cnt, 16)
        g.dma_start(out=flag_sb[:], in_=flag_dram).then_inc(pl.h, 16)
        g.wait_ge(pl.h, cnt)
        g.reg_load(fr, flag_sb[0:1, 0:1])
        g.br_cmp(fr, r_nonce, on_true=done, on_false=loop, comp_op="IS_EQ")
    nc.switch_bb(done)


def emit_handoff_x(k, t):
    nc = k.nc
    g = nc.gpsimd
    flag_sb = k.sb("hx_flag", [1, 1], I32)
    hx = k.ctr("hx")
    xc = k.ctr("hx_copy")
    if RACE_AUDIT[0]:
        hx.inc(g.dma_start(out=t["xp_sh"][:, 0:TOK], in_=t["xa_s"]), 16)
        hx.wait(g)
        xc.inc(g.dma_start(out=t["xl_s"][:, HALO:TOKH], in_=t["xp_sh"][:, 0:TOK]), 16)
        xc.inc(g.dma_start(
            out=t["xl_s"].rearrange("(kt p) n -> p kt n", p=128)[:, :, 0:HALO],
            in_=k.zero3[:].rearrange("p (kt n) -> p kt n", n=3)), 16)
        xc.wait(g)
        xr = k.gctr()
        xr.inc(g.memset(flag_sb[:], 0))
        return
    r = g.value_load(k.rolev[0:1, 0:1])
    with g.If(r):
        g.dma_start(out=t["xp_sh"][:, 0:TOK], in_=t["xa_s"]).then_inc(hx.h, 16)
        g.wait_ge(hx.h, 16)
        g.dma_start(out=t["fx_sh"], in_=k.noncev[0:1, 0:1]).then_inc(hx.h, 16)
        g.wait_ge(hx.h, 32)
    g.end_ifs()
    r2 = g.value_load(k.roleiv[0:1, 0:1])
    with g.If(r2):
        g.dma_start(out=t["xp_sh"][:, TOK:2 * TOK], in_=t["xa_s"]).then_inc(hx.h, 16)
        g.wait_ge(hx.h, 16)
    g.end_ifs()
    poll_flag(k, t["fx_sh"], flag_sb, 0, "pollx")
    r = g.value_load(k.rolev[0:1, 0:1])
    with g.If(r):
        g.dma_start(out=t["xl_s"][:, HALO:TOKH], in_=t["xp_sh"][:, 0:TOK]).then_inc(xc.h, 16)
        g.dma_start(out=t["xl_s"].rearrange("(kt p) n -> p kt n", p=128)[:, :, 0:HALO],
                    in_=k.zero3[:].rearrange("p (kt n) -> p kt n", n=3)).then_inc(xc.h, 16)
        g.wait_ge(xc.h, 32)
    g.end_ifs()
    r2 = g.value_load(k.roleiv[0:1, 0:1])
    with g.If(r2):
        g.dma_start(out=t["xl_s"][:, 0:TOKH],
                    in_=t["xp_sh"][:, TOK - HALO:2 * TOK]).then_inc(xc.h, 16)
        g.wait_ge(xc.h, 16)
    g.end_ifs()
    xr = k.gctr()
    xr.inc(g.memset(flag_sb[:], 0))



def emit_inproj(k, t, km):
    nc = k.nc
    k.dtc_fm = km.sb("m_dtcfm", [64, TOKH])
    k.lndt_fm = km.sb("m_lndtfm", [64, TOKH])
    k.dtcA_fm = km.sb("m_dtcAfm", [64, TOKH])
    with ExitStack() as fs:
        kk = K(nc, fs)
        sb = lambda n, s, d=F32: kk.sb("ip_" + n, s, d)
        xln = sb("xln", [128, 16 * TOKH], BF16)
        xtl = sb("xtl", [128, 2 * 512])
        wmr = sb("wmr", [128, RINGW * 2048], BF16)
        st_in = [sb(f"sin{i}", [128, TOKH]) for i in range(2)]
        st_out = [sb(f"sout{i}", [128, TOKH]) for i in range(2)]
        srow = sb("srow", [1, 3 * 512])
        rbm = sb("rbm", [128, 2 * 512])
        xbr = sb("xbr", [128, 512], BF16)

        xld = DmaRing(k, "ip_xld", 2)
        w_ld = DmaRing(k, "ip_wld", RINGW)
        vA = k.vctr()
        tA = k.ctr("ip_tA")
        sA = k.sctr()
        mmi = k.ctr("ip_mmi")
        evi = k.vctr()
        cvd = k.vctr()
        svd = k.sctr()
        stw = DmaRing(k, "ip_stw", 2)

        NT = [(0, 512), (512, 512), (1024, 512), (1536, 512), (2048, 3)]

        with ExitStack() as ps1:
            kq = K(nc, ps1)
            p_sx = kq.ps("ip_sx", [1, 512])
            p_sx2 = kq.ps("ip_sx2", [1, 512])
            p_rb = kq.ps("ip_rb", [128, 512])
            p_mrb = kq.ps("ip_mrb", [128, 512])
            xtl_free = [0, 0]
            for ni, (n0, w) in enumerate(NT):
                for kt in range(16):
                    par = kt % 2
                    nc.sync.wait_ge(k.vch.h, xtl_free[par])
                    xld.issue(nc.sync.dma_start(
                        out=xtl[:, par * 512:par * 512 + w],
                        in_=t["xl_s"][kt * 128:(kt + 1) * 128, n0:n0 + w]), par)
                    xld.wait(nc.vector, par)
                    tA.wait(nc.vector)
                    vA.inc(nc.vector.tensor_copy(out=xbr[:, 0:w],
                                                 in_=xtl[:, par * 512:par * 512 + w]))
                    xtl_free[par] = vA.n
                    vA.wait(nc.tensor)
                    tA.inc(nc.tensor.matmul(p_sx[0:1, 0:w], k.onecb[:, 0:1], xbr[:, 0:w],
                                            start=(kt == 0), stop=(kt == 15)))
                    tA.wait(nc.vector)
                    vA.inc(nc.vector.tensor_tensor(out=xbr[:, 0:w], in0=xbr[:, 0:w],
                                                   in1=xbr[:, 0:w], op=ALU.mult))
                    vA.wait(nc.tensor)
                    tA.inc(nc.tensor.matmul(p_sx2[0:1, 0:w], k.onecb[:, 0:1], xbr[:, 0:w],
                                            start=(kt == 0), stop=(kt == 15)))
                tA.wait(nc.vector)
                vA.inc(nc.vector.tensor_scalar(out=srow[0:1, 0:w], in0=p_sx[0:1, 0:w],
                                               scalar1=1.0 / D, scalar2=None, op0=ALU.mult))
                vA.inc(nc.vector.tensor_scalar(out=srow[0:1, 512:512 + w],
                                               in0=p_sx2[0:1, 0:w],
                                               scalar1=1.0 / D, scalar2=None, op0=ALU.mult))
                vA.inc(nc.vector.tensor_tensor(out=srow[0:1, 1024:1024 + w],
                                               in0=srow[0:1, 0:w], in1=srow[0:1, 0:w],
                                               op=ALU.mult))
                vA.inc(nc.vector.tensor_tensor(out=srow[0:1, 512:512 + w],
                                               in0=srow[0:1, 512:512 + w],
                                               in1=srow[0:1, 1024:1024 + w], op=ALU.subtract))
                vA.wait(nc.scalar)
                sA.inc(nc.scalar.activation(out=srow[0:1, 1024:1024 + w],
                                            in_=srow[0:1, 512:512 + w], func=AF.Ln, bias=k.cbias[0:1, 0:1]))
                sA.inc(nc.scalar.activation(out=srow[0:1, 512:512 + w],
                                            in_=srow[0:1, 1024:1024 + w], func=AF.Exp,
                                            scale=-0.5))
                sA.wait(nc.vector)
                vA.inc(nc.vector.tensor_tensor(out=srow[0:1, 1024:1024 + w],
                                               in0=srow[0:1, 0:w],
                                               in1=srow[0:1, 512:512 + w], op=ALU.mult))
                vA.inc(nc.vector.tensor_scalar(out=srow[0:1, 1024:1024 + w],
                                               in0=srow[0:1, 1024:1024 + w],
                                               scalar1=-1.0, scalar2=None, op0=ALU.mult))
                vA.wait(nc.tensor)
                tA.inc(nc.tensor.matmul(p_rb[:, 0:w], k.onesr[0:1, :],
                                        srow[0:1, 512:512 + w], start=True, stop=True))
                tA.inc(nc.tensor.matmul(p_mrb[:, 0:w], k.onesr[0:1, :],
                                        srow[0:1, 1024:1024 + w], start=True, stop=True))
                tA.wait(nc.vector)
                vA.inc(nc.vector.tensor_copy(out=rbm[:, 0:w], in_=p_rb[:, 0:w]))
                vA.inc(nc.vector.tensor_copy(out=rbm[:, 512:512 + w], in_=p_mrb[:, 0:w]))
                for kt in range(16):
                    par = kt % 2
                    nc.sync.wait_ge(k.vch.h, xtl_free[par])
                    xld.issue(nc.sync.dma_start(
                        out=xtl[:, par * 512:par * 512 + w],
                        in_=t["xl_s"][kt * 128:(kt + 1) * 128, n0:n0 + w]), par)
                    xld.wait(nc.vector, par)
                    vA.inc(nc.vector.tensor_tensor(out=xtl[:, par * 512:par * 512 + w],
                                                   in0=xtl[:, par * 512:par * 512 + w],
                                                   in1=rbm[:, 0:w], op=ALU.mult))
                    vA.inc(nc.vector.tensor_tensor(
                        out=xln[:, kt * TOKH + n0:kt * TOKH + n0 + w],
                        in0=xtl[:, par * 512:par * 512 + w],
                        in1=rbm[:, 512:512 + w], op=ALU.add))
                    xtl_free[par] = vA.n
        lnv = vA.n

        wi_mark = {}
        for m in range(NM_IP):
            if m >= RINGW:
                nc.gpsimd.wait_ge(mmi.h, 5 * (m - RINGW + 1))
            w_ld.issue(nc.gpsimd.dma_start(
                out=wmr[:, (m % RINGW) * 2048:(m % RINGW + 1) * 2048],
                in_=t["winw"][m]), m % RINGW)
            wi_mark[m] = w_ld.s[m % RINGW].n

        with ExitStack() as ps2:
            kq = K(nc, ps2)
            p_ip = [kq.ps(f"ip_p{i}", [128, 512]) for i in range(2)]
            evi_at = []
            last_g = [None, None]
            for m in range(NM_IP):
                w_ld.wait(nc.tensor, m % RINGW, wi_mark[m])
                if m == 0:
                    nc.tensor.wait_ge(vA.h, lnv)
                sbin = st_in[m % 2]
                if m >= 2:
                    stw.wait(nc.vector, m % 2)
                for ni, (n0, w) in enumerate(NT):
                    par = ni % 2
                    gidx = m * 5 + ni
                    if last_g[par] is not None:
                        nc.tensor.wait_ge(k.vch.h, evi_at[last_g[par]])
                    for kt in range(16):
                        mm = nc.tensor.matmul(
                            p_ip[par][:, 0:w],
                            wmr[:, (m % RINGW) * 2048 + kt * 128:(m % RINGW) * 2048 + (kt + 1) * 128],
                            xln[:, kt * TOKH + n0: kt * TOKH + n0 + w],
                            start=(kt == 0), stop=(kt == 15))
                    mmi.inc(mm)
                    nc.vector.wait_ge(mmi.h, gidx + 1)
                    evi.inc(nc.vector.tensor_copy(out=sbin[:, n0:n0 + w],
                                                  in_=p_ip[par][:, 0:w]))
                    evi_at.append(evi.n)
                    last_g[par] = gidx

                so = st_out[m % 2]
                if m < 32:
                    nc.scalar.wait_ge(evi.h, evi.n)
                    if m >= 2:
                        stw.wait(nc.scalar, m % 2)
                    svd.inc(nc.scalar.activation(out=so[:, :], in_=sbin[:, :], func=AF.Silu))
                    svd.wait(nc.sync)
                    stw.issue(nc.sync.dma_start(out=t["zs_s"][m * 128:(m + 1) * 128, :],
                                                in_=so[:, :]), m % 2)
                elif m < 66:
                    mc = m - 32
                    nc.vector.wait_ge(evi.h, evi.n)
                    cvd.inc(nc.vector.tensor_scalar(
                        out=so[:, HALO:TOKH], in0=sbin[:, 0:TOK],
                        scalar1=k.convw[:, mc * 4:mc * 4 + 1],
                        scalar2=k.convb[:, mc:mc + 1], op0=ALU.mult, op1=ALU.add))
                    for cc in range(1, 4):
                        cvd.inc(nc.vector.scalar_tensor_tensor(
                            out=so[:, HALO:TOKH], in0=sbin[:, cc:TOK + cc],
                            scalar=k.convw[:, mc * 4 + cc:mc * 4 + cc + 1],
                            in1=so[:, HALO:TOKH], op0=ALU.mult, op1=ALU.add))
                    cvd.inc(nc.vector.tensor_copy(out=so[:, 0:HALO], in_=k.zero3[:, 0:HALO]))
                    cvd.wait(nc.scalar)
                    svd.inc(nc.scalar.activation(out=so[:, HALO:TOKH], in_=so[:, HALO:TOKH],
                                                 func=AF.Silu))
                    svd.wait(nc.sync)
                    stw.issue(nc.sync.dma_start(out=t["xbc_s"][mc * 128:(mc + 1) * 128, :],
                                                in_=so[:, :]), m % 2)
                else:
                    nc.vector.wait_ge(evi.h, evi.n)
                    dtb2 = so[0:64, 0:TOKH]
                    cvd.inc(nc.vector.tensor_scalar(out=dtb2, in0=sbin[0:64, :],
                                                    scalar1=k.dtb[0:64, 0:1], scalar2=None,
                                                    op0=ALU.add))
                    cvd.wait(nc.scalar)
                    ab = st_in[(m + 1) % 2][0:64, 0:TOKH]
                    svd.inc(nc.scalar.activation(out=ab, in_=dtb2, func=AF.Abs))
                    svd.inc(nc.scalar.activation(out=ab, in_=ab, func=AF.Exp, scale=-1.0))
                    svd.inc(nc.scalar.activation(out=ab, in_=ab, func=AF.Ln, bias=k.cbias[0:64, 1:2]))
                    svd.wait(nc.vector)
                    cvd.inc(nc.vector.tensor_scalar_max(k.dtc_fm[:, :], dtb2, 0.0))
                    cvd.inc(nc.vector.tensor_add(out=k.dtc_fm[:, :], in0=k.dtc_fm[:, :],
                                                 in1=ab))
                    cvd.wait(nc.scalar)
                    svd.inc(nc.scalar.activation(out=k.lndt_fm[:, :], in_=k.dtc_fm[:, :],
                                                 func=AF.Ln))
                    cvd.inc(nc.vector.tensor_scalar(out=k.dtcA_fm[:, :], in0=k.dtc_fm[:, :],
                                                    scalar1=k.nA[0:64, 0:1], scalar2=None,
                                                    op0=ALU.mult))
                    cvd.wait(nc.sync)
                    svd.wait(nc.sync)
                    stw.issue(nc.sync.dma_start(out=t["xa2_s"][0:1, 0:8],
                                                in_=k.zero3[0:1, 0:8]), m % 2)
        barrier(k, "ipend", [stw.s[0], stw.s[1], cvd, svd, mmi, evi])



def emit_dt(k, t, km):
    nc = k.nc
    with ExitStack() as fs:
        kk = K(nc, fs)
        dtA_tm = kk.sb("dt_dtA", [128, 2 * 64])
        cfm = kk.sb("dt_cfm", [64, 2 * 128])
        dl = kk.sb("dt_dl", [64, 128])
        lnd_tm = kk.sb("dt_lnd", [128, 64])
        arow = kk.sb("dt_arow", [1, 64])
        p_t = kk.ps("dt_pt", [128, 128])
        p_c = kk.ps("dt_pc", [128, 64])
        tD = k.ctr("dt_t")
        vD = k.vctr()
        sD = k.sctr()
        dRow = k.ctr("dt_row")
        for c in range(NCK):
            b0 = HALO + c * CHK
            vD.wait(nc.tensor)
            tD.inc(nc.tensor.transpose(p_t[0:128, 0:64], k.dtc_fm[0:64, b0:b0 + 128],
                                       k.identf[0:64, 0:64]))
            tD.wait(nc.vector)
            vD.inc(nc.vector.tensor_copy(out=k.dtc_tm[:, c * 64:(c + 1) * 64],
                                         in_=p_t[0:128, 0:64]))
            vD.wait(nc.tensor)
            tD.inc(nc.tensor.transpose(p_t[0:128, 0:64], k.dtcA_fm[0:64, b0:b0 + 128],
                                       k.identf[0:64, 0:64]))
            tD.wait(nc.vector)
            vD.inc(nc.vector.tensor_copy(out=dtA_tm[:, (c % 2) * 64:(c % 2) * 64 + 64],
                                         in_=p_t[0:128, 0:64]))
            vD.wait(nc.tensor)
            tD.inc(nc.tensor.transpose(p_t[0:128, 0:64], k.lndt_fm[0:64, b0:b0 + 128],
                                       k.identf[0:64, 0:64]))
            tD.wait(nc.vector)
            vD.inc(nc.vector.tensor_copy(out=lnd_tm[:, :], in_=p_t[0:128, 0:64]))
            vD.wait(nc.tensor)
            tD.inc(nc.tensor.matmul(p_c[:, 0:64], k.utm[:, :],
                                    dtA_tm[:, (c % 2) * 64:(c % 2) * 64 + 64],
                                    start=True, stop=True))
            tD.wait(nc.vector)
            vD.inc(nc.vector.tensor_copy(out=k.cum_tm[:, c * 64:(c + 1) * 64],
                                         in_=p_c[:, 0:64]))
            vD.wait(nc.scalar)
            sD.inc(nc.scalar.activation(out=k.e_tm[:, c * 64:(c + 1) * 64],
                                        in_=k.cum_tm[:, c * 64:(c + 1) * 64], func=AF.Exp))
            vD.inc(nc.vector.tensor_copy(out=k.bias_tm[:, c * 64:(c + 1) * 64],
                                         in_=lnd_tm[:, :]))
            vD.wait(nc.tensor)
            tD.inc(nc.tensor.transpose(p_t[0:64, 0:128], k.cum_tm[:, c * 64:(c + 1) * 64],
                                       k.identf[:, :]))
            tD.wait(nc.vector)
            vD.inc(nc.vector.tensor_copy(out=cfm[0:64, (c % 2) * 128:(c % 2) * 128 + 128],
                                         in_=p_t[0:64, 0:128]))
            cslice = cfm[0:64, (c % 2) * 128:(c % 2) * 128 + 128]
            vD.wait(nc.scalar)
            sD.inc(nc.scalar.activation(out=dl[0:64, :], in_=cslice, func=AF.Exp,
                                        scale=-1.0,
                                        bias=cfm[0:64, (c % 2) * 128 + 127:(c % 2) * 128 + 128]))
            sD.wait(nc.tensor)
            tD.inc(nc.tensor.transpose(p_t[0:128, 0:64], dl[0:64, :], k.identf[0:64, 0:64]))
            tD.wait(nc.vector)
            vD.inc(nc.vector.tensor_copy(out=k.w_tm[:, c * 64:(c + 1) * 64],
                                         in_=p_t[0:128, 0:64]))
            vD.inc(nc.vector.tensor_mul(out=k.w_tm[:, c * 64:(c + 1) * 64],
                                        in0=k.w_tm[:, c * 64:(c + 1) * 64],
                                        in1=k.dtc_tm[:, c * 64:(c + 1) * 64]))
            vD.wait(nc.sync)
            sD.wait(nc.sync)
            dRow.inc(nc.sync.dma_start(
                out=arow[0:1, 0:64],
                in_=cfm[0:64, (c % 2) * 128 + 127:(c % 2) * 128 + 128]), 16)
            dRow.wait(nc.scalar)
            sD.inc(nc.scalar.activation(out=arow[0:1, :], in_=arow[0:1, :], func=AF.Exp))
            sD.wait(nc.tensor)
            vD.wait(nc.tensor)
            tD.inc(nc.tensor.matmul(p_c[:, 0:64], k.onesr[0:1, :], arow[0:1, :],
                                    start=True, stop=True))
            tD.wait(nc.vector)
            vD.inc(nc.vector.tensor_copy(out=k.atot[:, c * 64:(c + 1) * 64],
                                         in_=p_c[:, 0:64]))
        barrier(k, "dtend", [tD, vD, sD, dRow])



def emit_pass12(k, t, km):
    nc = k.nc
    with ExitStack() as fs:
        kk = K(nc, fs)
        sb = lambda n, s, d=F32: kk.sb("ss_" + n, s, d)
        prev = sb("prev", [128, NH * HD])
        xfb = sb("xfb", [128, DI])
        xfb_b = sb("xfbb", [128, DI], BF16)
        bfb = sb("bfb", [128, DS])
        bfb_b = sb("bfbb", [128, DS], BF16)
        xtm = sb("xtm", [128, DI], BF16)
        btm = sb("btm", [128, DS], BF16)
        xw = sb("xw", [128, 2 * HD], BF16)
        sflag = sb("sflag", [1, 1], I32)

        ld1 = k.ctr("p1_ld")
        t1 = k.ctr("p1_t")
        v1 = k.vctr()
        w1 = k.ctr("p1_w")

        with ExitStack() as ps1:
            kq = K(nc, ps1)
            p_tb = [kq.ps(f"p1_tb{i}", [128, 128], BF16) for i in range(2)]
            p_st = [kq.ps(f"p1_st{i}", [128, 64]) for i in range(2)]
            v1.inc(nc.vector.memset(prev[:], 0.0))
            for c in range(NCK):
                b0 = HALO + c * CHK
                if c > 0:
                    nc.sync.wait_ge(v1.h, v1.n)
                    nc.sync.wait_ge(w1.h, w1.n)
                ld1.inc(nc.sync.dma_start(
                    out=xfb[:].rearrange("p (kt n) -> p kt n", n=128),
                    in_=t["xbc_s"][0:DI, b0:b0 + 128].rearrange("(kt p) n -> p kt n", p=128)),
                    16)
                ld1.inc(nc.sync.dma_start(out=bfb[:], in_=t["xbc_s"][DI:DI + DS, b0:b0 + 128]),
                        16)
                ld1.wait(nc.vector)
                if c > 0:
                    nc.vector.wait_ge(t1.h, t1.n)
                v1.inc(nc.vector.tensor_copy(out=xfb_b[:], in_=xfb[:]))
                v1.inc(nc.vector.tensor_copy(out=bfb_b[:], in_=bfb[:]))
                for kt in range(33):
                    par = kt % 2
                    src_ap = xfb_b[:, kt * 128:(kt + 1) * 128] if kt < 32 else bfb_b[:, :]
                    nc.tensor.wait_ge(v1.h, v1.n - (1 if kt >= 2 else 0))
                    t1.inc(nc.tensor.transpose(p_tb[par][:, :], src_ap, k.identb[:, :]))
                    nc.vector.wait_ge(t1.h, t1.n)
                    dst_ap = xtm[:, kt * 128:(kt + 1) * 128] if kt < 32 else btm[:, :]
                    v1.inc(nc.vector.tensor_copy(out=dst_ap, in_=p_tb[par][:, :]))
                v1.wait(nc.sync)
                w1.inc(nc.sync.dma_start(out=t["xtm_s"][c], in_=xtm[:]), 16)
                w1.inc(nc.sync.dma_start(out=t["btm_s"][c], in_=btm[:]), 16)
                for h in range(NH):
                    par = h % 2
                    if h >= 2:
                        nc.vector.wait_ge(t1.h, t1.n)
                    v1.inc(nc.vector.tensor_scalar(
                        out=xw[:, par * HD:(par + 1) * HD], in0=xtm[:, h * HD:(h + 1) * HD],
                        scalar1=k.w_tm[:, c * 64 + h:c * 64 + h + 1], scalar2=None,
                        op0=ALU.mult))
                    nc.tensor.wait_ge(v1.h, v1.n)
                    t1.inc(nc.tensor.matmul(p_st[par][:, :], btm[:, :],
                                            xw[:, par * HD:(par + 1) * HD],
                                            start=True, stop=True))
                    nc.vector.wait_ge(t1.h, t1.n)
                    v1.inc(nc.vector.scalar_tensor_tensor(
                        out=prev[:, h * HD:(h + 1) * HD], in0=prev[:, h * HD:(h + 1) * HD],
                        scalar=k.atot[:, c * 64 + h:c * 64 + h + 1],
                        in1=p_st[par][:, :], op0=ALU.mult, op1=ALU.add))
                if DEBUG_DUMP[0]:
                    pd = k.ctr(f"p1_pd{c}")
                    v1.wait(nc.sync)
                    pd.inc(nc.sync.dma_start(out=t["pdbg_s"][c], in_=prev[:]), 16)
                    pd.wait(nc.vector)

        g = nc.gpsimd
        hs = k.ctr("hs")
        v1.wait(g)
        if RACE_AUDIT[0]:
            hs.inc(g.dma_start(out=t["st_sh"], in_=prev[:]), 16)
            hs.wait(g)
        else:
            r = g.value_load(k.rolev[0:1, 0:1])
            with g.If(r):
                g.dma_start(out=t["st_sh"], in_=prev[:]).then_inc(hs.h, 16)
                g.wait_ge(hs.h, 16)
                g.dma_start(out=t["fs_sh"], in_=k.noncev[0:1, 1:2]).then_inc(hs.h, 16)
                g.wait_ge(hs.h, 32)
            g.end_ifs()
            poll_flag(k, t["fs_sh"], sflag, 1, "polls")
        hr = k.ctr("hs_rd")
        hr.inc(g.dma_start(out=xfb[:, 0:NH * HD], in_=t["st_sh"]), 16)
        hr.wait(nc.vector)
        v1.inc(nc.vector.tensor_scalar(out=prev[:], in0=xfb[:, 0:NH * HD],
                                       scalar1=k.smask[:, 0:1], scalar2=None, op0=ALU.mult))

        tbuf = sb("tbuf", [128, 32 * 512], BF16)
        ytm = sb("ytm", [128, DI])
        tg = sb("tg", [128, DI], BF16)
        zch = sb("zch", [128, DI])
        xfm2 = sb("xfm2", [128, DI])
        cfm_f = sb("cfmf", [128, DS])
        cfm_b = sb("cfmb", [128, DS], BF16)
        cfm_b2 = sb("cfmb2", [128, DS], BF16)
        prev_b = xfb_b
        cbm = sb("cbm", [128, 128], BF16)
        colbc = sb("colbc", [128, 2 * 128])
        gsc = sb("gsc", [128, 128])
        rl = sb("rl", [128, 2 * 128])
        dec = sb("dec", [128, 2 * 128], BF16)
        ntb = sb("ntb", [128, 2 * 128], BF16)
        tsq = sb("tsq", [128, 128], BF16)
        y1 = sb("y1", [128, 2 * 64])
        rb2 = sb("rb2", [128, 128])
        rrow = sb("rrow", [1, 128])
        wo = sb("wo", [128, RINGO * DI], BF16)
        xpt = sb("xpt", [128, 2 * 512])
        x2st = sb("x2st", [128, 2 * 512])

        ld2 = k.ctr("p2_ld")
        ldz = k.ctr("p2_ldz")
        t2 = k.ctr("p2_t")
        v2 = k.vctr()
        s2 = k.sctr()
        gcb = k.gctr()
        tbc = k.ctr("p2_tbc")
        tmm3 = k.ctr("p2_m3")
        vhd = k.vctr()
        syd = k.sctr()
        srl = k.sctr()
        sdc = k.sctr()
        vnt = k.vctr()
        vxw = k.vctr()
        wol = DmaRing(k, "p2_wol", RINGO)
        top = k.ctr("p2_top")
        vop = k.vctr()
        wol_mark = {}
        vop_at = []
        xpl = DmaRing(k, "p2_xpl", 2)
        x2w = DmaRing(k, "p2_x2w", 2)
        dbgw = k.ctr("p2_dbgw") if DEBUG_DUMP[0] else None

        with ExitStack() as ps2:
            kq = K(nc, ps2)
            p_bc = [kq.ps(f"p2_bc{i}", [128, 128]) for i in range(2)]
            p_hd = [kq.ps(f"p2_hd{i}", [128, 192]) for i in range(2)]
            p_tf = [kq.ps(f"p2_tf{i}", [128, 128]) for i in range(2)]
            p_op = [kq.ps(f"p2_op{i}", [128, 512]) for i in range(2)]
            p_cb = p_tf[0]
            p_rm = p_bc[0][0:1, 0:128]

            for c in range(NCK):
                b0 = HALO + c * CHK
                sweep = c // 4
                if c == 0:
                    v1.wait(nc.sync)
                    t1.wait(nc.sync)
                    w1.wait(nc.sync)
                if c > 0:
                    for eng in (nc.sync,):
                        for cc_ in (vhd, v2, t2, s2):
                            cc_.wait(eng)
                    if dbgw is not None:
                        dbgw.wait(nc.vector)
                ld2.inc(nc.sync.dma_start(out=xtm[:], in_=t["xtm_s"][c]), 16)
                ld2.inc(nc.sync.dma_start(out=btm[:], in_=t["btm_s"][c]), 16)
                ld2.inc(nc.sync.dma_start(out=cfm_f[:],
                                          in_=t["xbc_s"][DI + DS:XBCD, b0:b0 + 128]), 16)
                ld2.inc(nc.sync.dma_start(out=bfb[:],
                                          in_=t["xbc_s"][DI:DI + DS, b0:b0 + 128]), 16)
                ld2.inc(nc.sync.dma_start(
                    out=xfm2[:].rearrange("p (kt n) -> p kt n", n=128),
                    in_=t["xbc_s"][0:DI, b0:b0 + 128].rearrange("(kt p) n -> p kt n", p=128)),
                    16)
                if c > 0:
                    v2.wait(nc.scalar)
                ldz.inc(nc.scalar.dma_start(
                    out=zch[:].rearrange("p (kt n) -> p kt n", n=128),
                    in_=t["zs_s"][0:DI, b0:b0 + 128].rearrange("(kt p) n -> p kt n", p=128)),
                    16)
                ld2.wait(nc.vector)
                ldz.wait(nc.vector)
                v2.inc(nc.vector.tensor_copy(out=bfb_b[:], in_=bfb[:]))
                v2.inc(nc.vector.tensor_copy(out=cfm_b[:], in_=cfm_f[:]))
                nc.tensor.wait_ge(v2.h, v2.n)
                if c > 0:
                    for cc_ in (vhd, syd, srl, sdc, vnt, vxw, s2):
                        cc_.wait(nc.tensor)
                t2.inc(nc.tensor.matmul(p_cb[:, :], bfb_b[:], cfm_b[:], start=True, stop=True))
                nc.vector.wait_ge(t2.h, t2.n)
                v2.inc(nc.vector.tensor_tensor(out=cbm[:, :], in0=p_cb[:, :], in1=k.utm[:, :],
                                               op=ALU.mult))
                if c > 0:
                    for eng in (nc.vector, nc.scalar, nc.gpsimd):
                        for cc_ in (tmm3, tbc, syd, srl, sdc, vnt, vxw, vhd):
                            cc_.wait(eng)
                if c == 0:
                    t1.wait(nc.vector)
                v2.inc(nc.vector.tensor_copy(out=prev_b[:], in_=prev[:]))
                v2.inc(nc.vector.tensor_copy(out=cfm_b2[:], in_=cfm_f[:]))
                srl_at, syd_at, vnt_at, vhdy_at, vhdp_at = [], [], [], [], []
                for h in range(NH):
                    par = h % 2
                    if h >= 2:
                        nc.gpsimd.wait_ge(tbc.h, tbc.n - 1)
                    gcb.inc(nc.gpsimd.tensor_scalar(
                        out=colbc[:, par * 128:(par + 1) * 128], in0=k.ones2d[:, :],
                        scalar1=k.cum_tm[:, c * 64 + h:c * 64 + h + 1], scalar2=None,
                        op0=ALU.mult))
                    nc.tensor.wait_ge(gcb.h, gcb.n)
                    if h >= 2:
                        nc.tensor.wait_ge(k.sch.h, srl_at[h - 2])
                    tbc.inc(nc.tensor.transpose(p_bc[par][:, :],
                                                colbc[:, par * 128:(par + 1) * 128],
                                                k.identf[:, :]))
                    nc.scalar.wait_ge(tbc.h, tbc.n)
                    if h >= 2:
                        nc.scalar.wait_ge(k.vch.h, vnt_at[h - 2])
                        nc.scalar.wait_ge(k.vch.h, vhdy_at[h - 2])
                    srl.inc(nc.scalar.activation(out=rl[:, par * 128:(par + 1) * 128],
                                                 in_=p_bc[par][:, :], func=AF.Relu, scale=-1.0,
                                                 bias=k.cum_tm[:, c * 64 + h:c * 64 + h + 1]))
                    srl_at.append(srl.n)
                    sdc.inc(nc.scalar.activation(out=dec[:, par * 128:(par + 1) * 128],
                                                 in_=rl[:, par * 128:(par + 1) * 128],
                                                 func=AF.Exp, scale=-1.0,
                                                 bias=k.bias_tm[:, c * 64 + h:c * 64 + h + 1]))
                    nc.vector.wait_ge(sdc.h, sdc.n)
                    if h >= 2:
                        nc.vector.wait_ge(tmm3.h, tmm3.n - 1)
                    vnt.inc(nc.vector.tensor_tensor(out=ntb[:, par * 128:(par + 1) * 128],
                                                    in0=cbm[:, :],
                                                    in1=dec[:, par * 128:(par + 1) * 128],
                                                    op=ALU.mult))
                    vnt_at.append(vnt.n)
                    vxw.inc(nc.vector.tensor_scalar(
                        out=xw[:, par * HD:(par + 1) * HD], in0=xtm[:, h * HD:(h + 1) * HD],
                        scalar1=k.w_tm[:, c * 64 + h:c * 64 + h + 1], scalar2=None,
                        op0=ALU.mult))
                    nc.tensor.wait_ge(vnt.h, vnt.n)
                    nc.tensor.wait_ge(vxw.h, vxw.n)
                    if h >= 2:
                        nc.tensor.wait_ge(k.sch.h, syd_at[h - 2])
                        nc.tensor.wait_ge(k.vch.h, vhdp_at[h - 2])
                    nc.tensor.matmul(p_hd[par][:, 0:64], ntb[:, par * 128:(par + 1) * 128],
                                     xtm[:, h * HD:(h + 1) * HD], start=True, stop=True)
                    nc.tensor.matmul(p_hd[par][:, 64:128], cfm_b2[:],
                                     prev_b[:, h * HD:(h + 1) * HD], start=True, stop=True)
                    tmm3.inc(nc.tensor.matmul(p_hd[par][:, 128:192], btm[:, :],
                                              xw[:, par * HD:(par + 1) * HD],
                                              start=True, stop=True))
                    nc.scalar.wait_ge(tmm3.h, tmm3.n)
                    syd.inc(nc.scalar.copy(out=y1[:, par * 64:(par + 1) * 64],
                                           in_=p_hd[par][:, 0:64]))
                    syd_at.append(syd.n)
                    nc.vector.wait_ge(syd.h, syd.n)
                    vhd.inc(nc.vector.scalar_tensor_tensor(
                        out=ytm[:, h * HD:(h + 1) * HD], in0=p_hd[par][:, 64:128],
                        scalar=k.e_tm[:, c * 64 + h:c * 64 + h + 1],
                        in1=y1[:, par * 64:(par + 1) * 64], op0=ALU.mult, op1=ALU.add))
                    vhdy_at.append(vhd.n)
                    vhd.inc(nc.vector.scalar_tensor_tensor(
                        out=prev[:, h * HD:(h + 1) * HD], in0=prev[:, h * HD:(h + 1) * HD],
                        scalar=k.atot[:, c * 64 + h:c * 64 + h + 1],
                        in1=p_hd[par][:, 128:192], op0=ALU.mult, op1=ALU.add))
                    vhdp_at.append(vhd.n)

                if dbgw is not None:
                    vhd.wait(nc.sync)
                    dbgw.inc(nc.sync.dma_start(out=t["ydbg_s"][c], in_=ytm[:]), 16)

                if c % 4 == 0 and c > 0:
                    nc.vector.wait_ge(top.h, top.n)
                for kt in range(32):
                    par = kt % 2
                    nc.tensor.wait_ge(vhd.h, vhd.n)
                    if kt >= 2:
                        nc.tensor.wait_ge(v2.h, v2.n)
                    t2.inc(nc.tensor.transpose(p_tf[par][:, :], ytm[:, kt * 128:(kt + 1) * 128],
                                               k.identf[:, :]))
                    nc.vector.wait_ge(t2.h, t2.n)
                    v2.inc(nc.vector.scalar_tensor_tensor(
                        out=gsc[:, :], in0=xfm2[:, kt * 128:(kt + 1) * 128],
                        scalar=k.dexp[:, kt:kt + 1], in1=p_tf[par][:, :],
                        op0=ALU.mult, op1=ALU.add))
                    v2.inc(nc.vector.tensor_tensor(out=tg[:, kt * 128:(kt + 1) * 128],
                                                   in0=gsc[:, :],
                                                   in1=zch[:, kt * 128:(kt + 1) * 128],
                                                   op=ALU.mult))
                    v2.inc(nc.vector.tensor_tensor(out=tsq[:, :],
                                                   in0=tg[:, kt * 128:(kt + 1) * 128],
                                                   in1=tg[:, kt * 128:(kt + 1) * 128],
                                                   op=ALU.mult))
                    nc.tensor.wait_ge(v2.h, v2.n)
                    t2.inc(nc.tensor.matmul(p_rm, k.onecb[:, 0:1], tsq[:, :],
                                            start=(kt == 0), stop=(kt == 31)))
                nc.scalar.wait_ge(t2.h, t2.n)
                s2.inc(nc.scalar.activation(out=rrow[0:1, :], in_=p_rm, func=AF.Ln,
                                            scale=1.0 / DI, bias=k.cbias[0:1, 0:1]))
                s2.inc(nc.scalar.activation(out=rrow[0:1, :], in_=rrow[0:1, :], func=AF.Exp,
                                            scale=-0.5))
                nc.tensor.wait_ge(s2.h, s2.n)
                t2.inc(nc.tensor.matmul(p_tf[0][:, :], k.onesr[0:1, :], rrow[0:1, :],
                                        start=True, stop=True))
                nc.vector.wait_ge(t2.h, t2.n)
                v2.inc(nc.vector.tensor_copy(out=rb2[:, :], in_=p_tf[0][:, :]))
                for kt in range(32):
                    v2.inc(nc.vector.scalar_tensor_tensor(
                        out=tbuf[:, kt * 512 + (c % 4) * 128: kt * 512 + (c % 4) * 128 + 128],
                        in0=tg[:, kt * 128:(kt + 1) * 128], scalar=k.nw[:, kt:kt + 1],
                        in1=rb2[:, :], op0=ALU.mult, op1=ALU.mult))

                if dbgw is not None and c % 4 == 3:
                    v2.wait(nc.sync)
                    dbgw.inc(nc.sync.dma_start(out=t["tdbg_s"][sweep], in_=tbuf[:]), 16)

                if c % 4 == 3:
                    for mb in range(16):
                        u = sweep * 16 + mb
                        if u >= RINGO:
                            nc.gpsimd.wait_ge(top.h, u - RINGO + 1)
                        wol.issue(nc.gpsimd.dma_start(
                            out=wo[:, (u % RINGO) * DI:(u % RINGO + 1) * DI],
                            in_=t["woutw"][mb]), u % RINGO)
                        wol_mark[u] = wol.s[u % RINGO].n
                    oc0 = sweep * 512
                    nc.tensor.wait_ge(v2.h, v2.n)
                    for mb in range(16):
                        u = sweep * 16 + mb
                        opar = mb % 2
                        wol.wait(nc.tensor, u % RINGO, wol_mark[u])
                        if u >= 2:
                            nc.sync.wait_ge(k.vch.h, vop_at[u - 2])
                        xpl.issue(nc.sync.dma_start(
                            out=xpt[:, opar * 512:(opar + 1) * 512],
                            in_=t["xl_s"][mb * 128:(mb + 1) * 128, HALO + oc0:HALO + oc0 + 512]),
                            opar)
                        if u >= 2:
                            nc.tensor.wait_ge(k.vch.h, vop_at[u - 2])
                        for kt in range(32):
                            mm = nc.tensor.matmul(
                                p_op[opar][:, :],
                                wo[:, (u % RINGO) * DI + kt * 128:(u % RINGO) * DI + (kt + 1) * 128],
                                tbuf[:, kt * 512:(kt + 1) * 512],
                                start=(kt == 0), stop=(kt == 31))
                        top.inc(mm)
                        nc.vector.wait_ge(top.h, top.n)
                        xpl.wait(nc.vector, opar)
                        if u >= 2:
                            x2w.wait(nc.vector, opar)
                        vop.inc(nc.vector.tensor_tensor(
                            out=x2st[:, opar * 512:(opar + 1) * 512],
                            in0=xpt[:, opar * 512:(opar + 1) * 512],
                            in1=p_op[opar][:, :], op=ALU.add))
                        vop_at.append(vop.n)
                        nc.sync.wait_ge(vop.h, vop.n)
                        x2w.issue(nc.sync.dma_start(
                            out=t["x2_s"][mb * 128:(mb + 1) * 128, oc0:oc0 + 512],
                            in_=x2st[:, opar * 512:(opar + 1) * 512]), opar)
        barrier(k, "p12", [t2, v2, s2, vhd, x2w.s[0], x2w.s[1], top, vop, w1, hr, gcb,
                           tmm3, ldz] + ([dbgw] if dbgw is not None else []))



_NC_CACHE = {}


def _host_prep(inputs):
    import ml_dtypes
    f32 = np.float32
    u = np.asarray(inputs["L0_attn_u"], f32)
    W_in = np.asarray(inputs["L0_attn_W_in"], f32)
    conv_w = np.asarray(inputs["L0_attn_conv_w"], f32)
    conv_b = np.asarray(inputs["L0_attn_conv_b"], f32)
    dt_bias = np.asarray(inputs["L0_attn_dt_bias"], f32)
    A_log = np.asarray(inputs["L0_attn_A_log"], f32)
    Dp = np.asarray(inputs["L0_attn_D"], f32)
    norm_w = np.asarray(inputs["L0_attn_norm_w"], f32)
    W_out = np.asarray(inputs["L0_attn_W_out"], f32)
    gate_w = np.asarray(inputs["L0_ffn_gate_w"], f32)
    mask_up = np.asarray(inputs["L0_ffn_mask_up_w"], f32)
    gproj = np.asarray(inputs["L0_ffn_gate_proj_w"], f32)
    uproj = np.asarray(inputs["L0_ffn_up_proj_w"], f32)
    dproj = np.asarray(inputs["L0_ffn_down_proj_w"], f32)
    inv = np.asarray(inputs["L0_ffn_inv_map"]).astype(np.int32)

    u_fm = np.ascontiguousarray(u.reshape(4 * 4096, D).T)

    def tile_gu(w):
        wt = w.transpose(0, 2, 1)
        return np.ascontiguousarray(
            wt.reshape(G, 16, 128, 11, 128).transpose(0, 3, 2, 1, 4).reshape(G, 11, 128, 2048))

    guw = np.ascontiguousarray(np.stack([tile_gu(gproj), tile_gu(uproj)], axis=2))
    dT = dproj.transpose(0, 2, 1)
    dnw = np.ascontiguousarray(
        dT.reshape(G, 11, 128, 16, 128).transpose(0, 3, 2, 1, 4).reshape(G, 16, 128, IFF))
    W_inp = np.zeros((D, DINP), f32)
    W_inp[:, :DIN] = W_in
    winw = np.ascontiguousarray(
        W_inp.reshape(16, 128, NM_IP, 128).transpose(2, 1, 0, 3).reshape(NM_IP, 128, 2048))
    woutw = np.ascontiguousarray(
        W_out.reshape(32, 128, 16, 128).transpose(2, 1, 0, 3).reshape(16, 128, DI))
    gatet = np.ascontiguousarray(
        gate_w.T.reshape(16, 128, E).transpose(1, 0, 2).reshape(128, 16 * E))
    mgt = np.ascontiguousarray(
        mask_up.reshape(G, 16, 128, GS).transpose(2, 0, 1, 3).reshape(128, G * 16 * GS))
    convw = np.ascontiguousarray(conv_w.reshape(34, 128, 4).transpose(1, 0, 2).reshape(128, 136))
    convb = np.ascontiguousarray(conv_b.reshape(34, 128).T)
    dexp = np.ascontiguousarray(np.repeat(Dp, HD).reshape(32, 128).T)
    nw = np.ascontiguousarray(norm_w.reshape(32, 128).T)

    bdiag = np.zeros((64, 8), f32)
    for f in range(64):
        bdiag[f, f // 8] = 1.0
    bdiagT = np.ascontiguousarray(bdiag.T)
    sel8 = np.zeros((8, G * 128), f32)
    for g in range(G):
        sel8[g, g * 128:(g + 1) * 128] = 0.5

    common = dict(
        guw=guw, dnw=dnw, winw=winw, woutw=woutw, gatet=gatet, mgt=mgt,
        convw=convw, convb=convb,
        dtb=dt_bias.reshape(NH, 1).copy(), alog=A_log.reshape(NH, 1).copy(),
        dexp=dexp, nw=nw, drow=Dp.reshape(1, NH).copy(),
        invm=inv.reshape(1, 64).copy(),
        identf=np.eye(128, dtype=f32),
        identb=np.eye(128, dtype=f32).astype(ml_dtypes.bfloat16),
        utm=(np.arange(128)[:, None] <= np.arange(128)[None, :]).astype(f32),
        ones2d=np.ones((128, 128), f32),
        onesr=np.ones((1, 128), f32),
        onecb=np.ones((128, 1), f32).astype(ml_dtypes.bfloat16),
        bdiag=bdiag, bdiagT=bdiagT, sel8=sel8,
    )
    return u_fm, common


def kernel(**inputs):
    u_fm, common = _host_prep(inputs)
    rng = np.random.default_rng()
    nonce = rng.integers(1 << 20, (1 << 30), size=2, dtype=np.int64).astype(np.int32)

    if "all" not in _NC_CACHE:
        _NC_CACHE["all"] = build("all")
    nc = _NC_CACHE["all"]

    in_maps = []
    for c in range(8):
        b, r = c // 2, c % 2
        m = dict(common)
        m["u"] = np.ascontiguousarray(u_fm[:, b * 4096 + r * TOK: b * 4096 + (r + 1) * TOK])
        m["role"] = np.array([[1 - r]], np.int32)
        m["rolei"] = np.array([[r]], np.int32)
        m["nonce"] = nonce.reshape(1, 2).copy()
        m["smask"] = np.full((128, 1), float(r), np.float32)
        in_maps.append(m)

    res = run_bass_kernel_spmd(nc, in_maps, core_ids=list(range(8)))
    cols = [res.results[c]["out"] for c in range(8)]
    full = np.concatenate(cols, axis=1)
    return np.ascontiguousarray(full.T).reshape(4, 4096, D).astype(np.float32)
